# version 03
# baseline (speedup 1.0000x reference)
import contextlib
import numpy as np
import concourse.bass as bass
import concourse.mybir as mybir
from concourse.bass_utils import run_bass_kernel_spmd

F32 = mybir.dt.float32
BF16 = mybir.dt.bfloat16
AF = mybir.ActivationFunctionType
ALU = mybir.AluOpType
AX = mybir.AxisListType

NCORES = 8
DM = 4096
NBLK = 32
TLOC = 2048
TC = 512
NCH = TLOC // TC
NST = 16
EPS = 1e-5
NQ = 2
QN = NST // NQ
N_DVE_U = 4


def build_nc(dbg=False):
    nc = bass.Bass("TRN2", target_bir_lowering=False, debug=False)

    x_h = nc.dram_tensor("x", [DM, TLOC], F32, kind="ExternalInput").ap()
    wxp_h = nc.dram_tensor("wxp", [DM, 160], F32, kind="ExternalInput").ap()
    wdt_h = nc.dram_tensor("wdt", [128, DM], F32, kind="ExternalInput").ap()
    alog_h = nc.dram_tensor("alog", [128, NBLK * NST], F32, kind="ExternalInput").ap()
    dvec_h = nc.dram_tensor("dvec", [128, NBLK], F32, kind="ExternalInput").ap()
    bvec_h = nc.dram_tensor("bvec", [128, NBLK], F32, kind="ExternalInput").ap()
    route_h = nc.dram_tensor("route", [DM, 8], F32, kind="ExternalInput").ap()
    affn_h = nc.dram_tensor("affn", [DM, 128], F32, kind="ExternalInput").ap()
    bwt_h = nc.dram_tensor("bwt", [128, DM], F32, kind="ExternalInput").ap()
    selbc_h = nc.dram_tensor("selbc", [32, 32 * 128], BF16, kind="ExternalInput").ap()
    seler_h = nc.dram_tensor("seler", [8, 128], F32, kind="ExternalInput").ap()
    ones1_h = nc.dram_tensor("ones1", [1, 128], F32, kind="ExternalInput").ap()
    selm_h = nc.dram_tensor("selm", [128, 8], F32, kind="ExternalInput").ap()
    invm_h = nc.dram_tensor("invm", [128, 8], F32, kind="ExternalInput").ap()
    out_h = nc.dram_tensor("o", [DM, TLOC], F32, kind="ExternalOutput").ap()

    cc_in = nc.dram_tensor("cc_in", [2 * DM, NST], F32)
    cc_out = nc.dram_tensor("cc_out", [2 * DM * NCORES, NST], F32)
    if dbg:
        dbgS_h = nc.dram_tensor("dbg_S", [128, NBLK * NST], F32,
                                kind="ExternalOutput").ap()
        dbgSc_h = [nc.dram_tensor(f"dbg_Sc{i}", [128, NBLK * NST], F32,
                                  kind="ExternalOutput").ap() for i in range(NCH)]
        dbgP_h = nc.dram_tensor("dbg_p", [128, NST * TC], BF16,
                                kind="ExternalOutput").ap()
        dbgU_h = nc.dram_tensor("dbg_u", [128, NST * TC], F32,
                                kind="ExternalOutput").ap()
        dbgA_h = nc.dram_tensor("dbg_a", [128, NST * TC], F32,
                                kind="ExternalOutput").ap()
        dbgDT_h = nc.dram_tensor("dbg_dt", [128, TC], F32,
                                 kind="ExternalOutput").ap()
        dbgDTX_h = nc.dram_tensor("dbg_dtx", [128, TC], F32,
                                  kind="ExternalOutput").ap()
        dbgB_h = nc.dram_tensor("dbg_B", [128, NST * TC], BF16,
                                kind="ExternalOutput").ap()
        dbgBC_h = nc.dram_tensor("dbg_bc", [32, TC], BF16,
                                 kind="ExternalOutput").ap()
        dbgDTI_h = nc.dram_tensor("dbg_dti", [128, TC], BF16,
                                  kind="ExternalOutput").ap()
        dbgSEL_h = nc.dram_tensor("dbg_sel", [32, 2048], BF16,
                                  kind="ExternalOutput").ap()
        dbgH_h = nc.dram_tensor("dbg_H", [128, NBLK * NST], F32,
                                kind="ExternalOutput").ap()

    es = contextlib.ExitStack()
    E = es.enter_context

    class Q:
        def __init__(self, name):
            self.sem = E(nc.semaphore(name))
            self.v = 0

        def inc(self, inst, n=1):
            inst.then_inc(self.sem, n)
            self.v += n
            return self.v

    q_ld = Q("q_ld")
    q_gd = Q("q_gd")
    q_pe = Q("q_pe")
    q_act = Q("q_act")
    q_dve = Q("q_dve")
    q_gp = Q("q_gp")
    q_cc = Q("q_cc")

    def W(eng, q, v):
        if v > 0:
            eng.wait_ge(q.sem, v)

    wdtb = E(nc.sbuf_tensor("wdtb", [128, NBLK, 128], BF16))
    negA = E(nc.sbuf_tensor("negA", [128, NBLK, NST], F32))
    oneD = E(nc.sbuf_tensor("oneD", [128, NBLK], F32))
    bvec = E(nc.sbuf_tensor("bvec_s", [128, NBLK], F32))
    selB = E(nc.sbuf_tensor("selB_s", [32, 16, 128], BF16))
    selC = E(nc.sbuf_tensor("selC_s", [32, 16, 128], BF16))
    seler = E(nc.sbuf_tensor("seler_s", [8, 128], F32))
    ones1 = E(nc.sbuf_tensor("ones1_s", [1, 128], F32))
    onesk = E(nc.sbuf_tensor("onesk_s", [128, 1], F32))
    selm = E(nc.sbuf_tensor("selm_s", [128, 8], F32))
    invm = E(nc.sbuf_tensor("invm_s", [128, 8], F32))
    Hin = E(nc.sbuf_tensor("Hin", [128, NBLK, NST], F32))
    bc_all = E(nc.sbuf_tensor("bc_all", [32, NCH, TC], BF16))
    dtin_all = E(nc.sbuf_tensor("dtin_all", [128, NCH, TC], BF16))
    B_bc = E(nc.sbuf_tensor("B_bc", [128, NST, TC], BF16))
    C_bc = E(nc.sbuf_tensor("C_bc", [128, NST, TC], BF16))
    p_bf = E(nc.sbuf_tensor("p_bf", [128, NST, TC], BF16))
    dA_q = [E(nc.sbuf_tensor(f"dA_q{i}", [128, NQ, TC], F32)) for i in range(2)]
    u2 = [E(nc.sbuf_tensor(f"u2_{i}", [128, NQ, TC], F32)) for i in range(2)]
    dt_rot = [E(nc.sbuf_tensor(f"dt{i}", [128, TC], F32)) for i in range(2)]
    dtx_rot = [E(nc.sbuf_tensor(f"dtx{i}", [128, TC], F32)) for i in range(2)]
    sp1 = E(nc.sbuf_tensor("sp1", [128, TC], F32))
    xf_sc = [E(nc.sbuf_tensor(f"xf_sc{i}", [128, TC], F32)) for i in range(2)]
    rs_acc = E(nc.sbuf_tensor("rs_acc", [128, 4], F32))
    tmp44 = E(nc.sbuf_tensor("tmp44", [128, NST], F32))
    y_rot = [E(nc.sbuf_tensor("y0", [128, TC], F32))] * 2

    pb = [E(nc.psum_tensor(f"pb{i}", [128, TC], F32)) for i in range(8)]

    esW = contextlib.ExitStack()
    EW = esW.enter_context
    wxpb = EW(nc.sbuf_tensor("wxpb", [128, NBLK, 160], BF16, side="right"))
    xf_xp = xf_sc
    xb_rot = [EW(nc.sbuf_tensor("xb0", [128, TC], BF16, side="right"))] * 2
    hcarryA = EW(nc.sbuf_tensor("hcarryA", [128, NBLK, NST], F32, side="right"))
    rcarry = EW(nc.sbuf_tensor("rcarry", [128, NBLK], F32, side="right"))
    dvec_s = EW(nc.sbuf_tensor("dvec_s", [128, NBLK], F32, side="right"))
    Pc = EW(nc.sbuf_tensor("Pc", [128, NBLK, NST], F32, side="right"))

    blk = E(nc.Block())

    with nc.sbuf_tensor("wstg", [128, NBLK, 160], F32, side="right") as stg:
        def load_cast(src_ap, ncols, dst, dst_cols):
            @blk.sync
            def _(sync):
                W(sync, q_dve, q_dve.v)
                q_ld.inc(sync.dma_start(out=stg[:, :, 0:ncols], in_=src_ap), 16)

            @blk.vector
            def _(v):
                W(v, q_ld, q_ld.v)
                q_dve.inc(v.tensor_copy(dst[:, :, 0:dst_cols],
                                        stg[:, :, 0:ncols]))

        load_cast(wxp_h.rearrange("(blk p) m -> p blk m", p=128), 160, wxpb, 160)
        load_cast(wdt_h.rearrange("r (blk m) -> r blk m", m=128), 128, wdtb, 128)

        @blk.sync
        def _(sync):
            W(sync, q_dve, q_dve.v)
            q_ld.inc(sync.dma_start(
                out=stg[:, :, 0:NST],
                in_=alog_h.rearrange("p (blk n) -> p blk n", n=NST)), 16)
            q_ld.inc(sync.dma_start(out=dvec_s[:], in_=dvec_h), 16)
            q_ld.inc(sync.dma_start(out=bvec[:], in_=bvec_h), 16)
            q_ld.inc(sync.dma_start(
                out=selB[:],
                in_=selbc_h[:, 0:2048].rearrange("k (j m) -> k j m", m=128)), 16)
            q_ld.inc(sync.dma_start(
                out=selC[:],
                in_=selbc_h[:, 2048:4096].rearrange("k (j m) -> k j m", m=128)), 16)
            q_ld.inc(sync.dma_start(out=seler[:], in_=seler_h), 16)
            q_ld.inc(sync.dma_start(out=ones1[:], in_=ones1_h), 16)
            q_ld.inc(sync.dma_start(out=selm[:], in_=selm_h), 16)
            q_ld.inc(sync.dma_start(out=invm[:], in_=invm_h), 16)

        @blk.scalar
        def _(s):
            W(s, q_ld, q_ld.v)
            q_act.inc(s.activation(negA[:], stg[:, :, 0:NST], AF.Exp))
            W(s, q_act, q_act.v)
            q_act.inc(s.activation(oneD[:], dvec_s[:], AF.Copy, bias=1.0))

        @blk.vector
        def _(v):
            W(v, q_act, q_act.v)
            q_dve.inc(v.tensor_scalar(negA[:], negA[:], -1.0, None, ALU.mult))
            W(v, q_dve, q_dve.v)
            q_dve.inc(v.memset(hcarryA[:], 0.0))
            q_dve.inc(v.memset(rcarry[:], 0.0))
            q_dve.inc(v.memset(onesk[:], 1.0))
            W(v, q_dve, q_dve.v)

    bar0 = (q_ld.v, q_act.v, q_dve.v)
    for sec in (blk.sync, blk.scalar, blk.vector, blk.tensor, blk.gpsimd):
        @sec
        def _(e):
            e.wait_ge(q_ld.sem, bar0[0])
            e.wait_ge(q_act.sem, bar0[1])
            e.wait_ge(q_dve.sem, bar0[2])

    st_ = {
        "cast": [0] * NBLK,
        "xpmm": [0] * NBLK,
        "bb_evac": [0, 0, 0, 0],
        "dtx": [0] * NBLK,
        "u_dve": [0] * NBLK,
        "z": [0] * NBLK,
        "sp_rd": [0, 0],
        "dt_done": [0] * NBLK,
        "gp_last": 0,
        "dA_rd_last": 0,
        "uq": [0, 0],
        "carry": [0] * NBLK,
        "evac_prev": 0,
        "ubc_dve": 0,
        "ubc_gp": 0,
        "stats_rd": 0,
        "stats_pe": 0,
        "lgps": 0,
        "x1b_used": [0] * NBLK,
        "sq_g": [0],
        "out_st": [0] * NBLK,
        "out_ap": [0] * NBLK,
        "bc45_free": 0,
        "ln_ap": [0] * NBLK,
        "rb_evac": 0,
    }

    hb = {}

    def emit_xp_sweep(c):
        for b in range(NBLK):
            r2 = b % 2

            @blk.sync
            def _(sync):
                W(sync, q_dve, st_["cast"][(b - 2) % NBLK])
                W(sync, q_gp, st_["dtx"][(b - 2) % NBLK])
                W(sync, q_dve, st_["u_dve"][(b - 2) % NBLK])
                W(sync, q_dve, st_["z"][(b - 2) % NBLK])
                W(sync, q_gd, st_["out_st"][(b - 2) % NBLK])
                q_ld.inc(sync.dma_start(
                    out=xf_xp[r2][:],
                    in_=x_h[b * 128:(b + 1) * 128, c * TC:(c + 1) * TC]), 16)
            ld1 = q_ld.v

            @blk.vector
            def _(v):
                W(v, q_ld, ld1)
                W(v, q_pe, st_["xpmm"][(b - 1) % NBLK])
                q_dve.inc(v.tensor_copy(xb_rot[0][:], xf_xp[r2][:]))
            st_["cast"][b] = q_dve.v

            @blk.tensor
            def _(t):
                W(t, q_dve, st_["cast"][b])
                if b == 0:
                    W(t, q_act, st_["evac_prev"])
                t.matmul(pb[6][:], wxpb[:, b, 0:128], xb_rot[0][:],
                         start=(b == 0), stop=(b == NBLK - 1))
                q_pe.inc(t.matmul(pb[7][0:32, :], wxpb[:, b, 128:160],
                                  xb_rot[0][:],
                                  start=(b == 0), stop=(b == NBLK - 1)))
            st_["xpmm"][b] = q_pe.v
        xp_all = q_pe.v

        @blk.scalar
        def _(s):
            W(s, q_pe, xp_all)
            s.copy(dtin_all[:, c, :], pb[6][:])
            q_act.inc(s.copy(bc_all[:, c, :], pb[7][0:32, :]))
        st_["evac_prev"] = q_act.v

    def emit_bcast(c, with_C):
        dep = st_["evac_prev"]
        js = list(range(16)) + (list(range(16, 32)) if with_C else [])
        pend = []
        for idx, j in enumerate(js):
            bank = 2 + idx % 4

            @blk.tensor
            def _(t):
                W(t, q_act, dep)
                W(t, q_act, st_["bb_evac"][idx % 4])
                if idx == 0:
                    W(t, q_dve, st_["ubc_dve"])
                    W(t, q_gp, st_["ubc_gp"])
                sel = selC[:, j - 16, :] if j >= 16 else selB[:, j, :]
                q_pe.inc(t.matmul(pb[bank][:], sel, bc_all[:, c, :],
                                  start=True, stop=True))
            pend.append((j, bank, idx % 4, q_pe.v))
            if len(pend) == 4 or idx == len(js) - 1:
                @blk.scalar
                def _(s, pend=list(pend)):
                    inst = None
                    for (jj, bk, sl, pv) in pend:
                        W(s, q_pe, pv)
                        dst = C_bc[:, jj - 16, :] if jj >= 16 else B_bc[:, jj, :]
                        inst = s.copy(dst, pb[bk][:])
                    q_act.inc(inst)
                for (_, _, sl, _) in pend:
                    st_["bb_evac"][sl] = q_act.v
                pend = []
        return q_act.v

    def emit_front(c, b, phase, bdone):
        r2 = b % 2

        @blk.sync
        def _(sync):
            W(sync, q_gp, st_["dtx"][(b - 2) % NBLK])
            W(sync, q_dve, st_["u_dve"][(b - 2) % NBLK])
            W(sync, q_dve, st_["cast"][(b - 2) % NBLK])
            if phase == "B":
                W(sync, q_dve, st_["z"][(b - 2) % NBLK])
                W(sync, q_gd, st_["out_st"][(b - 2) % NBLK])
            q_ld.inc(sync.dma_start(
                out=xf_sc[r2][:],
                in_=x_h[b * 128:(b + 1) * 128, c * TC:(c + 1) * TC]), 16)
        ld2 = q_ld.v

        @blk.tensor
        def _(t):
            if b == 0:
                W(t, q_act, bdone)
            W(t, q_act, st_["sp_rd"][r2])
            q_pe.inc(t.matmul(pb[r2][:], wdtb[:, b, :], dtin_all[:, c, :],
                              start=True, stop=True))
        dtmm = q_pe.v

        @blk.scalar
        def _(s):
            W(s, q_pe, dtmm)
            W(s, q_gp, st_["dt_done"][(b - 1) % NBLK])
            W(s, q_gp, st_["dtx"][(b - 2) % NBLK])
            if dbg and phase == "A" and c == 0 and b == 2:
                W(s, q_gd, st_.get("dbg_hold2", 0))
            a1 = s.activation(sp1[:], pb[r2][:], AF.Abs, bias=bvec[:, b:b + 1])
            q_act.inc(a1)
            v_abs = q_act.v
            q_act.inc(s.activation(dt_rot[r2][:], pb[r2][:], AF.Relu,
                                   bias=bvec[:, b:b + 1],
                                   accum_out=rs_acc[:, r2:r2 + 1]))
            W(s, q_act, v_abs)
            q_act.inc(s.activation(pb[r2][:], sp1[:], AF.Exp, scale=-1.0))
            W(s, q_act, q_act.v)
            q_act.inc(s.activation(sp1[:], pb[r2][:], AF.Ln, bias=1.0,
                                   accum_out=rs_acc[:, 2 + r2:3 + r2]))
            st_["sp_rd"][r2] = q_act.v
        sp_done = q_act.v

        @blk.gpsimd
        def _(g):
            W(g, q_act, sp_done)
            i1 = g.tensor_tensor(dt_rot[r2][:], dt_rot[r2][:], sp1[:], ALU.add)
            if phase == "A":
                q_gp.inc(g.tensor_tensor(rs_acc[:, r2:r2 + 1],
                                         rs_acc[:, r2:r2 + 1],
                                         rs_acc[:, 2 + r2:3 + r2], ALU.add))
                W(g, q_gp, q_gp.v)
                q_gp.inc(g.tensor_tensor(rcarry[:, b:b + 1],
                                         rcarry[:, b:b + 1],
                                         rs_acc[:, r2:r2 + 1], ALU.add))
            else:
                q_gp.inc(i1)
            W(g, q_ld, ld2)
            W(g, q_dve, st_["u_dve"][(b - 2) % NBLK])
            q_gp.inc(g.tensor_tensor(dtx_rot[r2][:], dt_rot[r2][:],
                                     xf_sc[r2][:], ALU.mult))
        st_["dt_done"][b] = q_gp.v
        st_["dtx"][b] = q_gp.v
        if dbg and phase == "A" and c == 0 and b == 0:
            gp_now = q_gp.v

            @blk.gpsimd
            def _(g):
                W(g, q_gp, gp_now)
                q_gd.inc(g.dma_start(out=dbgDT_h, in_=dt_rot[0][:]), 16)
                q_gd.inc(g.dma_start(out=dbgDTX_h, in_=dtx_rot[0][:]), 16)
                q_gd.inc(g.dma_start(
                    out=dbgB_h, in_=B_bc[:].rearrange("p a t -> p (a t)")), 16)
                q_gd.inc(g.dma_start(out=dbgBC_h, in_=bc_all[:, 0, :]), 16)
                q_gd.inc(g.dma_start(
                    out=dbgSEL_h,
                    in_=selB[:].rearrange("k j m -> k (j m)")), 16)
                q_gd.inc(g.dma_start(out=dbgDTI_h, in_=dtin_all[:, 0, :]), 16)
            st_["dbg_hold2"] = q_gd.v

    def emit_scan_work(pc, pbk, phase):
        pr2 = pbk % 2
        carry_t = hcarryA if phase == "A" else Hin

        dA_pair_done = 0
        scan_after = [0] * QN
        for qq in range(QN):
            ubuf = qq % 2
            if qq % 2 == 0:
                @blk.scalar
                def _(s, qq=qq):
                    W(s, q_dve, st_["dA_rd_last"] if qq == 0
                      else scan_after[qq - 1])
                    if qq == 0:
                        W(s, q_gp, st_["dt_done"][pbk])
                    inst = None
                    for i in range(2 * NQ):
                        n = qq * NQ + i
                        inst = s.activation(dA_q[i // NQ][:, i % NQ, :],
                                            dt_rot[pr2][:], AF.Exp,
                                            scale=negA[:, pbk, n:n + 1])
                    q_act.inc(inst)
                dA_pair_done = q_act.v

            @blk.gpsimd
            def _(g):
                W(g, q_gp, st_["dtx"][pbk])
                W(g, q_dve, st_["uq"][ubuf])
                inst = None
                for i in range(NQ):
                    n = qq * NQ + i
                    if n >= NST - N_DVE_U:
                        continue
                    inst = g.tensor_tensor(u2[ubuf][:, i, :], dtx_rot[pr2][:],
                                           B_bc[:, n, :], ALU.mult)
                if inst is not None:
                    q_gp.inc(inst)
            u_gp = q_gp.v

            @blk.vector
            def _(v):
                W(v, q_gp, u_gp)
                for i in range(NQ):
                    n = qq * NQ + i
                    if n < NST - N_DVE_U:
                        continue
                    W(v, q_dve, st_["uq"][ubuf])
                    q_dve.inc(v.tensor_tensor(u2[ubuf][:, i, :],
                                              dtx_rot[pr2][:],
                                              B_bc[:, n, :], ALU.mult))
                st_["u_dve"][pbk] = q_dve.v
                W(v, q_act, dA_pair_done)
                dA_sl = dA_q[qq % 2][:]
                n0 = qq * NQ
                q_dve.inc(v.tensor_tensor(
                    tmp44[:, 0:NQ],
                    dA_sl[:, :, 0:1].rearrange("p a o -> p (a o)"),
                    carry_t[:, pbk, n0:n0 + NQ], ALU.mult))
                W(v, q_dve, q_dve.v)
                q_dve.inc(v.tensor_tensor(
                    u2[ubuf][:, :, 0:1].rearrange("p a o -> p (a o)"),
                    u2[ubuf][:, :, 0:1].rearrange("p a o -> p (a o)"),
                    tmp44[:, 0:NQ], ALU.add))
                W(v, q_dve, q_dve.v)
                q_dve.inc(v.memset(
                    dA_sl[:, :, 0:1].rearrange("p a o -> p (a o)"), 0.0))
                W(v, q_dve, q_dve.v)
                q_dve.inc(v.tensor_tensor_scan(
                    p_bf[:, n0:n0 + NQ, :].rearrange("p a t -> p (a t)"),
                    dA_sl.rearrange("p a t -> p (a t)"),
                    u2[ubuf][:].rearrange("p a t -> p (a t)"),
                    0.0, ALU.mult, ALU.add))
                st_["uq"][ubuf] = q_dve.v
                scan_after[qq] = q_dve.v
                if qq == QN - 1:
                    st_["dA_rd_last"] = q_dve.v

        if dbg and phase == "A" and pc == 0 and pbk == 0:
            dve_now = q_dve.v

            @blk.gpsimd
            def _(g):
                W(g, q_dve, dve_now)
                q_gd.inc(g.dma_start(
                    out=dbgP_h, in_=p_bf[:].rearrange("p a t -> p (a t)")), 16)
                q_gd.inc(g.dma_start(
                    out=dbgU_h[:, 0:2 * TC],
                    in_=u2[1][:].rearrange("p a t -> p (a t)")), 16)
                q_gd.inc(g.dma_start(
                    out=dbgA_h[:, 0:2 * TC],
                    in_=dA_q[1][:].rearrange("p a t -> p (a t)")), 16)
            st_["dbg_hold"] = q_gd.v

        @blk.vector
        def _(v):
            if dbg and phase == "A" and pc == 0 and pbk == 0:
                W(v, q_gd, st_["dbg_hold"])
            W(v, q_dve, q_dve.v)
            q_dve.inc(v.tensor_copy(
                carry_t[:, pbk, :],
                p_bf[:, :, TC - 1:TC].rearrange("p a o -> p (a o)")))
        st_["carry"][pbk] = q_dve.v

        if phase == "B":
            zS = hb["zS"]
            sq_rot = hb["sq_rot"]

            @blk.vector
            def _(v):
                W(v, q_dve, q_dve.v)
                q_dve.inc(v.tensor_tensor(
                    p_bf[:].rearrange("p a t -> p (a t)"),
                    p_bf[:].rearrange("p a t -> p (a t)"),
                    C_bc[:].rearrange("p a t -> p (a t)"), ALU.mult))
                W(v, q_dve, q_dve.v)
                q_dve.inc(v.tensor_tensor(
                    p_bf[:, 0:8, :].rearrange("p a t -> p (a t)"),
                    p_bf[:, 0:8, :].rearrange("p a t -> p (a t)"),
                    p_bf[:, 8:16, :].rearrange("p a t -> p (a t)"), ALU.add))
                W(v, q_dve, q_dve.v)
                q_dve.inc(v.tensor_tensor(
                    p_bf[:, 0:4, :].rearrange("p a t -> p (a t)"),
                    p_bf[:, 0:4, :].rearrange("p a t -> p (a t)"),
                    p_bf[:, 4:8, :].rearrange("p a t -> p (a t)"), ALU.add))
                W(v, q_dve, q_dve.v)
                q_dve.inc(v.tensor_tensor(
                    p_bf[:, 0:2, :].rearrange("p a t -> p (a t)"),
                    p_bf[:, 0:2, :].rearrange("p a t -> p (a t)"),
                    p_bf[:, 2:4, :].rearrange("p a t -> p (a t)"), ALU.add))
                W(v, q_dve, q_dve.v)
                q_dve.inc(v.tensor_tensor(y_rot[pbk % 2][:], p_bf[:, 0, :],
                                          p_bf[:, 1, :], ALU.add))
                W(v, q_dve, q_dve.v)
                W(v, q_dve, st_["out_ap"][pbk])
                W(v, q_pe, st_["stats_pe"])
                q_dve.inc(v.scalar_tensor_tensor(
                    zS[:, pbk, :], xf_sc[pr2][:], oneD[:, pbk:pbk + 1],
                    y_rot[pbk % 2][:], ALU.mult, ALU.add))
            st_["z"][pbk] = q_dve.v

            @blk.gpsimd
            def _(g):
                W(g, q_gp, st_["sq_g"][0])
                W(g, q_dve, st_["z"][pbk])
                q_gp.inc(g.tensor_tensor(sq_rot[:], zS[:, pbk, :],
                                         zS[:, pbk, :], ALU.mult))
            st_["sq_g"][0] = q_gp.v
            sq_here = q_gp.v

            @blk.tensor
            def _(t):
                W(t, q_gp, sq_here)
                if pbk == 0:
                    W(t, q_dve, st_["stats_rd"])
                t.matmul(pb[6][0:1, :], onesk[:], zS[:, pbk, :],
                         start=(pbk == 0), stop=(pbk == NBLK - 1))
                q_pe.inc(t.matmul(pb[7][0:1, :], onesk[:], sq_rot[:],
                                  start=(pbk == 0), stop=(pbk == NBLK - 1)))
            if pbk == NBLK - 1:
                st_["stats_pe"] = q_pe.v

    for c in range(NCH):
        emit_xp_sweep(c)
        bdone = emit_bcast(c, with_C=False)
        for b in range(NBLK):
            emit_front(c, b, "A", bdone)
            if b > 0:
                emit_scan_work(c, b - 1, "A")
        emit_scan_work(c, NBLK - 1, "A")
        st_["ubc_dve"] = q_dve.v
        st_["ubc_gp"] = q_gp.v
        if dbg:
            dve_now = q_dve.v

            @blk.gpsimd
            def _(g, c=c, dve_now=dve_now):
                W(g, q_dve, dve_now)
                q_gd.inc(g.dma_start(
                    out=dbgSc_h[c],
                    in_=hcarryA[:].rearrange("p a b -> p (a b)")), 16)

    @blk.scalar
    def _(s):
        W(s, q_gp, q_gp.v)
        inst = None
        for b in range(NBLK):
            inst = s.activation(Pc[:, b, :], negA[:, b, :], AF.Exp,
                                scale=rcarry[:, b:b + 1])
        q_act.inc(inst)
    pc_done = q_act.v
    hc_all = max(st_["carry"])

    @blk.gpsimd
    def _(g):
        W(g, q_act, pc_done)
        W(g, q_dve, hc_all)
        q_cc.inc(g.dma_start(
            out=cc_in[0:DM, :].rearrange("(blk p) n -> p blk n", p=128),
            in_=hcarryA[:]), 16)
        q_cc.inc(g.dma_start(
            out=cc_in[DM:2 * DM, :].rearrange("(blk p) n -> p blk n", p=128),
            in_=Pc[:]), 16)
        g.wait_ge(q_cc.sem, q_cc.v)
        q_cc.inc(g.collective_compute(
            "AllGather", ALU.bypass,
            replica_groups=[list(range(NCORES))],
            ins=[cc_in.ap().opt()], outs=[cc_out.ap().opt()]), 1)
        g.wait_ge(q_cc.sem, q_cc.v)
        if dbg:
            q_gd.inc(g.dma_start(
                out=dbgS_h, in_=hcarryA[:].rearrange("p a b -> p (a b)")), 16)

    barA = (q_ld.v, q_gd.v, q_pe.v, q_act.v, q_dve.v, q_gp.v, q_cc.v)
    for sec in (blk.sync, blk.scalar, blk.vector, blk.tensor, blk.gpsimd):
        @sec
        def _(e):
            e.wait_ge(q_ld.sem, barA[0])
            e.wait_ge(q_gd.sem, barA[1])
            e.wait_ge(q_pe.sem, barA[2])
            e.wait_ge(q_act.sem, barA[3])
            e.wait_ge(q_dve.sem, barA[4])
            e.wait_ge(q_gp.sem, barA[5])
            e.wait_ge(q_cc.sem, barA[6])

    esW.close()

    esG = contextlib.ExitStack()
    gS = esG.enter_context(nc.sbuf_tensor("gS", [128, NCORES, NBLK, NST], F32,
                                           side="right"))
    gP = esG.enter_context(nc.sbuf_tensor("gP", [128, NCORES, NBLK, NST], F32,
                                          side="right"))
    pprime = esG.enter_context(nc.sbuf_tensor("pprime", [128, NBLK * NST], F32,
                                              side="right"))

    @blk.gpsimd
    def _(g):
        for j in range(NCORES):
            base = j * 2 * DM
            q_gd.inc(g.dma_start(
                out=gS[:, j, :, :],
                in_=cc_out[base:base + DM, :].rearrange(
                    "(blk p) n -> p blk n", p=128)), 16)
            q_gd.inc(g.dma_start(
                out=gP[:, j, :, :],
                in_=cc_out[base + DM:base + 2 * DM, :].rearrange(
                    "(blk p) n -> p blk n", p=128)), 16)
    gather_done = q_gd.v

    @blk.vector
    def _(v):
        W(v, q_gd, gather_done)
        q_dve.inc(v.memset(Hin[:], 0.0))
        Hf = Hin[:].rearrange("p a b -> p (a b)")
        for j in range(NCORES):
            gPj = gP[:, j, :, :].rearrange("p a b -> p (a b)")
            gSj = gS[:, j, :, :].rearrange("p a b -> p (a b)")
            W(v, q_dve, q_dve.v)
            q_dve.inc(v.tensor_scalar(pprime[:], gPj, selm[:, j:j + 1],
                                      invm[:, j:j + 1], ALU.mult, ALU.add))
            W(v, q_dve, q_dve.v)
            q_dve.inc(v.tensor_tensor(Hf, Hf, pprime[:], ALU.mult))
            W(v, q_dve, q_dve.v)
            q_dve.inc(v.tensor_scalar(pprime[:], gSj, selm[:, j:j + 1], None,
                                      ALU.mult))
            W(v, q_dve, q_dve.v)
            q_dve.inc(v.tensor_tensor(Hf, Hf, pprime[:], ALU.add))
        W(v, q_dve, q_dve.v)
    hin_done = q_dve.v

    if dbg:
        @blk.gpsimd
        def _(g):
            W(g, q_dve, hin_done)
            q_gd.inc(g.dma_start(
                out=dbgH_h, in_=Hin[:].rearrange("p a b -> p (a b)")), 16)

    barB = (q_gd.v, q_dve.v)
    for sec in (blk.sync, blk.scalar, blk.vector, blk.tensor, blk.gpsimd):
        @sec
        def _(e):
            e.wait_ge(q_gd.sem, barB[0])
            e.wait_ge(q_dve.sem, barB[1])

    esG.close()

    esB = contextlib.ExitStack()
    EB = esB.enter_context
    hb["zS"] = EB(nc.sbuf_tensor("zS", [128, NBLK, TC], F32))
    hb["sq_rot"] = EB(nc.sbuf_tensor("sq0", [128, TC], F32))
    routeb = EB(nc.sbuf_tensor("routeb", [128, NBLK, 8], BF16))
    affnb = EB(nc.sbuf_tensor("affnb", [128, NBLK, 128], BF16))
    bwtb = EB(nc.sbuf_tensor("bwtb", [128, NBLK, 128], BF16))
    stat_s = EB(nc.sbuf_tensor("stat_s", [1, 3, TC], F32))
    x1b_rot = [EB(nc.sbuf_tensor("x1b0", [128, TC], BF16))] * 2
    lg_s = EB(nc.sbuf_tensor("lg_s", [8, TC], F32))
    tmp8 = EB(nc.sbuf_tensor("tmp8", [8, TC], F32))
    hf_s = EB(nc.sbuf_tensor("hf_s", [128, TC], F32))
    wh_s = EB(nc.sbuf_tensor("wh_s", [128, TC], BF16))
    out_rot = xf_sc
    zS = hb["zS"]
    sq_rot = hb["sq_rot"]

    for (wsrc, ncols, wdst) in (
        (affn_h.rearrange("(blk p) m -> p blk m", p=128), 128, affnb),
        (bwt_h.rearrange("k (blk m) -> k blk m", m=128), 128, bwtb),
        (route_h.rearrange("(blk p) m -> p blk m", p=128), 8, routeb),
    ):
        @blk.sync
        def _(sync, wsrc=wsrc, ncols=ncols):
            W(sync, q_dve, q_dve.v)
            q_ld.inc(sync.dma_start(out=zS[:, :, 0:ncols], in_=wsrc), 16)

        @blk.vector
        def _(v, ncols=ncols, wdst=wdst):
            W(v, q_ld, q_ld.v)
            q_dve.inc(v.tensor_copy(wdst[:, :, 0:ncols], zS[:, :, 0:ncols]))

    def stats_math():
        @blk.vector
        def _(v):
            W(v, q_pe, st_["stats_pe"])
            W(v, q_dve, q_dve.v)
            q_dve.inc(v.tensor_scalar(stat_s[:, 0, :], pb[6][0:1, :],
                                      1.0 / DM, None, ALU.mult))
            W(v, q_dve, q_dve.v)
            q_dve.inc(v.tensor_scalar(stat_s[:, 1, :], pb[7][0:1, :],
                                      1.0 / DM, None, ALU.mult))
            W(v, q_dve, q_dve.v)
            q_dve.inc(v.tensor_tensor(stat_s[:, 2, :], stat_s[:, 0, :],
                                      stat_s[:, 0, :], ALU.mult))
            W(v, q_dve, q_dve.v)
            q_dve.inc(v.tensor_tensor(stat_s[:, 1, :], stat_s[:, 1, :],
                                      stat_s[:, 2, :], ALU.subtract))
            W(v, q_dve, q_dve.v)
            q_dve.inc(v.tensor_scalar(stat_s[:, 1, :], stat_s[:, 1, :], EPS,
                                      None, ALU.add))
        var_v = q_dve.v

        @blk.scalar
        def _(s):
            W(s, q_dve, var_v)
            q_act.inc(s.activation(stat_s[:, 2, :], stat_s[:, 1, :], AF.Ln))
            W(s, q_act, q_act.v)
            q_act.inc(s.activation(stat_s[:, 2, :], stat_s[:, 2, :], AF.Exp,
                                   scale=-0.5))
        rstd_v = q_act.v

        @blk.vector
        def _(v):
            W(v, q_act, rstd_v)
            W(v, q_dve, q_dve.v)
            q_dve.inc(v.tensor_tensor(stat_s[:, 1, :], stat_s[:, 0, :],
                                      stat_s[:, 2, :], ALU.mult))
        mr_v = q_dve.v

        @blk.tensor
        def _(t):
            W(t, q_dve, mr_v)
            W(t, q_dve, st_["bc45_free"])
            t.matmul(pb[4][:], ones1[:], stat_s[:, 2, :], start=True, stop=True)
            q_pe.inc(t.matmul(pb[5][:], ones1[:], stat_s[:, 1, :],
                              start=True, stop=True))
        bc_pe = q_pe.v
        return mr_v, bc_pe

    for c in range(NCH):
        bdone = emit_bcast(c, with_C=True)
        for b in range(NBLK):
            emit_front(c, b, "B", bdone)
            if b > 0:
                emit_scan_work(c, b - 1, "B")
        emit_scan_work(c, NBLK - 1, "B")

        stats_rd1, bc1 = stats_math()
        st_["stats_rd"] = stats_rd1

        for b in range(NBLK):
            @blk.vector
            def _(v):
                W(v, q_pe, bc1)
                W(v, q_dve, q_dve.v)
                q_dve.inc(v.tensor_tensor(zS[:, b, :], zS[:, b, :], pb[4][:],
                                          ALU.mult))
                W(v, q_dve, q_dve.v)
                q_dve.inc(v.tensor_tensor(zS[:, b, :], zS[:, b, :], pb[5][:],
                                          ALU.subtract))
                W(v, q_pe, st_["x1b_used"][(b - 1) % NBLK])
                W(v, q_dve, q_dve.v)
                q_dve.inc(v.tensor_copy(x1b_rot[b % 2][:], zS[:, b, :]))
            x1b_here = q_dve.v
            if b == NBLK - 1:
                st_["bc45_free"] = q_dve.v

            @blk.tensor
            def _(t):
                W(t, q_dve, x1b_here)
                if b == 0:
                    W(t, q_act, st_["lgps"])
                t.matmul(pb[2][0:8, :], routeb[:, b, :], x1b_rot[b % 2][:],
                         start=(b == 0), stop=(b == NBLK - 1))
                q_pe.inc(t.matmul(pb[3][:], affnb[:, b, :], x1b_rot[b % 2][:],
                                  start=(b == 0), stop=(b == NBLK - 1)))
            st_["x1b_used"][b] = q_pe.v
        ffn_in_pe = q_pe.v

        @blk.scalar
        def _(s):
            W(s, q_pe, ffn_in_pe)
            s.copy(lg_s[:], pb[2][0:8, :])
            q_act.inc(s.copy(hf_s[:], pb[3][:]))
        lg_evac = q_act.v
        st_["lgps"] = lg_evac

        @blk.gpsimd
        def _(g):
            W(g, q_act, lg_evac)
            q_gd.inc(g.dma_start(out=tmp8[0:4, :], in_=lg_s[4:8, :]), 16)
        t4a = q_gd.v

        @blk.vector
        def _(v):
            W(v, q_gd, t4a)
            q_dve.inc(v.tensor_tensor(tmp8[0:4, :], lg_s[0:4, :], tmp8[0:4, :],
                                      ALU.max))
        m4d = q_dve.v

        @blk.gpsimd
        def _(g):
            W(g, q_dve, m4d)
            q_gd.inc(g.dma_start(out=y_rot[0][0:2, :], in_=tmp8[2:4, :]), 16)
        t4b = q_gd.v

        @blk.vector
        def _(v):
            W(v, q_gd, t4b)
            W(v, q_dve, q_dve.v)
            q_dve.inc(v.tensor_tensor(tmp8[0:2, :], tmp8[0:2, :],
                                      y_rot[0][0:2, :], ALU.max))
        m2d = q_dve.v

        @blk.gpsimd
        def _(g):
            W(g, q_dve, m2d)
            q_gd.inc(g.dma_start(out=y_rot[0][0:1, :], in_=tmp8[1:2, :]), 16)
        t4c = q_gd.v

        @blk.vector
        def _(v):
            W(v, q_gd, t4c)
            W(v, q_dve, q_dve.v)
            q_dve.inc(v.tensor_tensor(tmp8[0:1, :], tmp8[0:1, :],
                                      y_rot[0][0:1, :], ALU.max))
        m1d = q_dve.v

        @blk.tensor
        def _(t):
            W(t, q_dve, m1d)
            q_pe.inc(t.matmul(pb[2][0:8, :], ones1[:, 0:8], tmp8[0:1, :],
                              start=True, stop=True))
        m8pe = q_pe.v

        @blk.vector
        def _(v):
            W(v, q_pe, m8pe)
            W(v, q_dve, q_dve.v)
            q_dve.inc(v.tensor_tensor(tmp8[:], lg_s[:], pb[2][0:8, :],
                                      ALU.is_equal))
            W(v, q_dve, q_dve.v)
            q_dve.inc(v.tensor_tensor(tmp8[:], tmp8[:], lg_s[:], ALU.mult))
        rw8d = q_dve.v

        @blk.tensor
        def _(t):
            W(t, q_dve, rw8d)
            q_pe.inc(t.matmul(pb[3][:], seler[:], tmp8[:], start=True,
                              stop=True))
        mask_pe = q_pe.v

        @blk.vector
        def _(v):
            W(v, q_pe, mask_pe)
            W(v, q_dve, q_dve.v)
            q_dve.inc(v.tensor_tensor(wh_s[:], hf_s[:], pb[3][:], ALU.mult))
        wh_made = q_dve.v

        z2_par = [0, 0]
        for b in range(NBLK):
            bank = 4 + b % 2

            @blk.tensor
            def _(t):
                W(t, q_dve, wh_made)
                W(t, q_dve, z2_par[b % 2])
                if b < 2:
                    W(t, q_dve, st_["bc45_free"])
                q_pe.inc(t.matmul(pb[bank][:], bwtb[:, b, :], wh_s[:],
                                  start=True, stop=True))
            fo_pe = q_pe.v

            @blk.vector
            def _(v):
                W(v, q_pe, fo_pe)
                W(v, q_dve, q_dve.v)
                q_dve.inc(v.tensor_tensor(zS[:, b, :], zS[:, b, :],
                                          pb[bank][:], ALU.add))
            z2_par[b % 2] = q_dve.v
            z2_here = q_dve.v

            @blk.gpsimd
            def _(g):
                W(g, q_dve, z2_here)
                W(g, q_gp, st_["sq_g"][0])
                q_gp.inc(g.tensor_tensor(sq_rot[:], zS[:, b, :], zS[:, b, :],
                                         ALU.mult))
            st_["sq_g"][0] = q_gp.v
            sq2_here = q_gp.v

            @blk.tensor
            def _(t):
                W(t, q_gp, sq2_here)
                if b == 0:
                    W(t, q_dve, stats_rd1)
                t.matmul(pb[6][0:1, :], onesk[:], zS[:, b, :],
                         start=(b == 0), stop=(b == NBLK - 1))
                q_pe.inc(t.matmul(pb[7][0:1, :], onesk[:], sq_rot[:],
                                  start=(b == 0), stop=(b == NBLK - 1)))
            if b == NBLK - 1:
                st_["stats_pe"] = q_pe.v

        stats_rd2, bc2 = stats_math()
        st_["stats_rd"] = stats_rd2

        for b in range(NBLK):
            @blk.vector
            def _(v):
                W(v, q_pe, bc2)
                W(v, q_gd, st_["out_st"][(b - 2) % NBLK])
                W(v, q_dve, q_dve.v)
                q_dve.inc(v.tensor_tensor(out_rot[b % 2][:], zS[:, b, :],
                                          pb[4][:], ALU.mult))
                W(v, q_dve, q_dve.v)
                q_dve.inc(v.tensor_tensor(out_rot[b % 2][:], out_rot[b % 2][:],
                                          pb[5][:], ALU.subtract))
            st_["out_ap"][b] = q_dve.v
            o_here = q_dve.v

            @blk.gpsimd
            def _(g):
                W(g, q_dve, o_here)
                q_gd.inc(g.dma_start(
                    out=out_h[b * 128:(b + 1) * 128, c * TC:(c + 1) * TC],
                    in_=out_rot[b % 2][:]), 16)
            st_["out_st"][b] = q_gd.v
        st_["ubc_dve"] = q_dve.v
        st_["ubc_gp"] = q_gp.v

    @blk.gpsimd
    def _(g):
        g.wait_ge(q_gd.sem, q_gd.v)

    esB.close()
    es.close()
    return nc


def _host_prep(L0_attn_x, L0_attn_A_log, L0_attn_D, L0_attn_x_proj_w,
               L0_attn_dt_proj_w, L0_attn_dt_proj_b, L0_ffn_route_w, L0_ffn_A,
               L0_ffn_Bw):
    import ml_dtypes
    x = np.asarray(L0_attn_x, dtype=np.float32)
    A_log = np.asarray(L0_attn_A_log, dtype=np.float32)
    Dv = np.asarray(L0_attn_D, dtype=np.float32)
    xp_w = np.asarray(L0_attn_x_proj_w, dtype=np.float32)
    dt_w = np.asarray(L0_attn_dt_proj_w, dtype=np.float32)
    dt_b = np.asarray(L0_attn_dt_proj_b, dtype=np.float32)
    route_w = np.asarray(L0_ffn_route_w, dtype=np.float32)
    A_ffn = np.asarray(L0_ffn_A, dtype=np.float32)
    Bw = np.asarray(L0_ffn_Bw, dtype=np.float32)

    wxp = np.ascontiguousarray(xp_w.T)
    wdt = np.ascontiguousarray(dt_w.T)
    alog = np.ascontiguousarray(
        A_log.reshape(NBLK, 128, NST).transpose(1, 0, 2)).reshape(
        128, NBLK * NST)
    dvec = np.ascontiguousarray(Dv.reshape(NBLK, 128).T)
    bvec = np.ascontiguousarray(dt_b.reshape(NBLK, 128).T)
    route = np.ascontiguousarray(route_w.T)
    affn = np.ascontiguousarray(np.transpose(A_ffn, (2, 0, 1)).reshape(DM, 128))
    bwt = np.ascontiguousarray(np.transpose(Bw, (0, 2, 1)).reshape(128, DM))
    selbc = np.zeros((32, 32 * 128), dtype=ml_dtypes.bfloat16)
    for j in range(32):
        selbc[j, j * 128:(j + 1) * 128] = 1.0
    seler = np.zeros((8, 128), dtype=np.float32)
    for e in range(8):
        seler[e, e * 16:(e + 1) * 16] = 1.0
    ones1 = np.ones((1, 128), dtype=np.float32)

    in_maps = []
    for cid in range(NCORES):
        bb, qq = cid // 4, cid % 4
        xc = np.ascontiguousarray(x[bb, qq * TLOC:(qq + 1) * TLOC, :].T)
        sel = np.zeros(8, dtype=np.float32)
        for j in range(NCORES):
            if j // 4 == bb and j % 4 < qq:
                sel[j] = 1.0
        in_maps.append({
            "x": xc, "wxp": wxp, "wdt": wdt, "alog": alog, "dvec": dvec,
            "bvec": bvec, "route": route, "affn": affn, "bwt": bwt,
            "selbc": selbc, "seler": seler, "ones1": ones1,
            "selm": np.ascontiguousarray(np.tile(sel, (128, 1))),
            "invm": np.ascontiguousarray(np.tile(1.0 - sel, (128, 1))),
        })
    return in_maps


def kernel(L0_attn_x, L0_attn_A_log, L0_attn_D, L0_attn_x_proj_w,
           L0_attn_dt_proj_w, L0_attn_dt_proj_b, L0_ffn_route_w, L0_ffn_A,
           L0_ffn_Bw):
    in_maps = _host_prep(L0_attn_x, L0_attn_A_log, L0_attn_D, L0_attn_x_proj_w,
                         L0_attn_dt_proj_w, L0_attn_dt_proj_b, L0_ffn_route_w,
                         L0_ffn_A, L0_ffn_Bw)
    nc = build_nc()
    res = run_bass_kernel_spmd(nc, in_maps, core_ids=list(range(NCORES)))
    out = np.empty((2, 8192, DM), dtype=np.float32)
    for cid in range(NCORES):
        bb, qq = cid // 4, cid % 4
        out[bb, qq * TLOC:(qq + 1) * TLOC, :] = res.results[cid]["o"].T
    return out


# version 09
# speedup vs baseline: 1.2373x; 1.2373x over previous
import contextlib
import numpy as np
import concourse.bass as bass
import concourse.mybir as mybir
from concourse.bass_utils import run_bass_kernel_spmd

F32 = mybir.dt.float32
BF16 = mybir.dt.bfloat16
AF = mybir.ActivationFunctionType
ALU = mybir.AluOpType

NCORES = 8
DM = 4096
NBLK = 32
TLOC = 2048
TC = 512
NCH = TLOC // TC
NST = 16
EPS = 1e-5
NQ = 2
QN = NST // NQ
NDA = 3
NUR = 2
DVEQ_A = 1

DMAQ = ("ld", "st", "sd", "gd", "cc")


def build_nc(dbg=False):
    nc = bass.Bass("TRN2", target_bir_lowering=False, debug=False)

    x_h = nc.dram_tensor("x", [DM, TLOC], F32, kind="ExternalInput").ap()
    wxp_h = nc.dram_tensor("wxp", [DM, 160], F32, kind="ExternalInput").ap()
    wdt_h = nc.dram_tensor("wdt", [128, DM], F32, kind="ExternalInput").ap()
    alog_h = nc.dram_tensor("alog", [128, NBLK * NST], F32, kind="ExternalInput").ap()
    dvec_h = nc.dram_tensor("dvec", [128, NBLK], F32, kind="ExternalInput").ap()
    bvec_h = nc.dram_tensor("bvec", [128, NBLK], F32, kind="ExternalInput").ap()
    route_h = nc.dram_tensor("route", [DM, 8], F32, kind="ExternalInput").ap()
    affn_h = nc.dram_tensor("affn", [DM, 128], F32, kind="ExternalInput").ap()
    bwt_h = nc.dram_tensor("bwt", [128, DM], F32, kind="ExternalInput").ap()
    selbc_h = nc.dram_tensor("selbc", [32, 32 * 128], BF16, kind="ExternalInput").ap()
    seler_h = nc.dram_tensor("seler", [8, 128], F32, kind="ExternalInput").ap()
    ones1_h = nc.dram_tensor("ones1", [1, 128], F32, kind="ExternalInput").ap()
    selm_h = nc.dram_tensor("selm", [128, 8], F32, kind="ExternalInput").ap()
    invm_h = nc.dram_tensor("invm", [128, 8], F32, kind="ExternalInput").ap()
    out_h = nc.dram_tensor("o", [DM, TLOC], F32, kind="ExternalOutput").ap()

    cc_in = nc.dram_tensor("cc_in", [2 * DM, NST], F32)
    cc_out = nc.dram_tensor("cc_out", [2 * DM * NCORES, NST], F32)
    ccf_in = nc.dram_tensor("ccf_in", [1, NST], F32)
    ccf_out = nc.dram_tensor("ccf_out", [1, NST], F32)
    if dbg:
        dbgS_h = nc.dram_tensor("dbg_S", [128, NBLK * NST], F32,
                                kind="ExternalOutput").ap()
        dbgSc_h = [nc.dram_tensor(f"dbg_Sc{i}", [128, NBLK * NST], F32,
                                  kind="ExternalOutput").ap() for i in range(NCH)]
        dbgH_h = nc.dram_tensor("dbg_H", [128, NBLK * NST], F32,
                                kind="ExternalOutput").ap()
        dbgDT_h = nc.dram_tensor("dbg_dt", [128, TC], F32,
                                 kind="ExternalOutput").ap()
        dbgDTX_h = nc.dram_tensor("dbg_dtx", [128, TC], BF16,
                                  kind="ExternalOutput").ap()
        dbgB_h = nc.dram_tensor("dbg_B", [128, NST * TC], BF16,
                                kind="ExternalOutput").ap()
        dbgP_h = nc.dram_tensor("dbg_p", [128, NST * TC], BF16,
                                kind="ExternalOutput").ap()
        dbgZ_h = nc.dram_tensor("dbg_z", [128, NBLK * TC], F32,
                                kind="ExternalOutput").ap()
        dbgX1_h = nc.dram_tensor("dbg_x1", [128, NBLK * TC], F32,
                                 kind="ExternalOutput").ap()
        dbgLG_h = nc.dram_tensor("dbg_lg", [8, TC], F32,
                                 kind="ExternalOutput").ap()
        dbgRW_h = nc.dram_tensor("dbg_rw", [8, TC], F32,
                                 kind="ExternalOutput").ap()
        dbgWH_h = nc.dram_tensor("dbg_wh", [128, TC], F32,
                                 kind="ExternalOutput").ap()
        dbgZ2_h = nc.dram_tensor("dbg_z2", [128, NBLK * TC], F32,
                                 kind="ExternalOutput").ap()
        dbgOD_h = nc.dram_tensor("dbg_od", [128, NBLK], F32,
                                 kind="ExternalOutput").ap()
        dbgODm_h = [nc.dram_tensor(f"dbg_od{i}", [128, NBLK], F32,
                                   kind="ExternalOutput").ap()
                    for i in range(5)]
        dbgDV_h = nc.dram_tensor("dbg_dv", [128, NBLK], F32,
                                 kind="ExternalOutput").ap()

    es = contextlib.ExitStack()
    E = es.enter_context

    zS = E(nc.sbuf_tensor("zS", [128, NBLK, TC], F32))
    p_bf = E(nc.sbuf_tensor("p_bf", [128, NST, TC], BF16))
    B_bc = E(nc.sbuf_tensor("B_bc", [128, NST, TC], BF16))
    C_bc = E(nc.sbuf_tensor("C_bc", [128, NST, TC], BF16))
    dA_q = [E(nc.sbuf_tensor(f"dA{i}", [128, NQ, TC], F32)) for i in range(NDA)]
    u_q = [E(nc.sbuf_tensor(f"u{i}", [128, NQ, TC], F32)) for i in range(NUR)]
    dtin_all = E(nc.sbuf_tensor("dtin_all", [128, NCH, TC], BF16))
    bc_all = E(nc.sbuf_tensor("bc_all", [32, NCH, TC], BF16))
    wdtb = E(nc.sbuf_tensor("wdtb", [128, NBLK, 128], BF16))
    selB = E(nc.sbuf_tensor("selB_s", [32, 16, 128], BF16))
    selC = E(nc.sbuf_tensor("selC_s", [32, 16, 128], BF16))
    routeb = E(nc.sbuf_tensor("routeb", [128, NBLK, 8], BF16))
    seler = E(nc.sbuf_tensor("seler_s", [8, 128], F32))
    ones1 = E(nc.sbuf_tensor("ones1_s", [1, 128], F32))
    onesk = E(nc.sbuf_tensor("onesk_s", [128, 1], F32))
    selm = E(nc.sbuf_tensor("selm_s", [128, 8], F32))
    invm = E(nc.sbuf_tensor("invm_s", [128, 8], F32))
    negA = E(nc.sbuf_tensor("negA", [128, NBLK, NST], F32))
    oneD = E(nc.sbuf_tensor("oneD", [128, NBLK], F32))
    dvec_s = E(nc.sbuf_tensor("dvec_s", [128, NBLK], F32))
    bvec = E(nc.sbuf_tensor("bvec_s", [128, NBLK], F32))
    Hin = E(nc.sbuf_tensor("Hin", [128, NBLK, NST], F32))
    dt_rot = [E(nc.sbuf_tensor(f"dt{i}", [128, TC], F32)) for i in range(2)]
    dtx_rot = [E(nc.sbuf_tensor(f"dtx{i}", [128, TC], F32)) for i in range(2)]
    sp1_rot = [E(nc.sbuf_tensor(f"sp{i}", [128, TC], F32)) for i in range(2)]
    xf_rot = [E(nc.sbuf_tensor(f"xf{i}", [128, TC], F32)) for i in range(2)]
    sq_rot = [E(nc.sbuf_tensor("sq0", [128, TC], F32))] * 2
    lg_s = E(nc.sbuf_tensor("lg_s", [8, TC], F32))
    tmp8 = E(nc.sbuf_tensor("tmp8", [8, TC], F32))
    hf_s = E(nc.sbuf_tensor("hf_s", [128, TC], F32))
    rs_acc = E(nc.sbuf_tensor("rs_acc", [128, 4], F32))
    inj = E(nc.sbuf_tensor("inj", [128, 32], F32))
    xsb_rot = [E(nc.sbuf_tensor("xsb0", [128, TC], BF16))] * 2
    x1b_rot = [E(nc.sbuf_tensor(f"x1b{i}", [128, TC], BF16)) for i in range(2)]
    whb = E(nc.sbuf_tensor("whb", [128, TC], BF16))

    esW1 = contextlib.ExitStack()
    EW1 = esW1.enter_context
    wxpb = EW1(nc.sbuf_tensor("wxpb", [128, NBLK, 160], BF16, side="right"))
    xs_rot = [EW1(nc.sbuf_tensor(f"xs{i}", [128, TC], F32, side="right"))
              for i in range(2)]
    hcarryA = EW1(nc.sbuf_tensor("hcarryA", [128, NBLK, NST], F32, side="right"))
    rcarry = EW1(nc.sbuf_tensor("rcarry", [128, NBLK], F32, side="right"))
    Pc = xs_rot[0][:].rearrange("p (a n) -> p a n", n=NST)

    pb = [E(nc.psum_tensor(f"pb{i}", [128, TC], F32)) for i in range(8)]

    st_m = lg_s[0:1, :]
    st_v = tmp8[0:1, :]
    st_sc = hf_s[0:1, :]

    ENG = ("sync", "tensor", "scalar", "vector", "gpsimd")
    streams = {e: [] for e in ENG}
    cnt = {}
    waited = {e: {} for e in ENG}
    selfv = {e: {} for e in ENG}
    EV0 = None

    def op(eng, emit, waits=(), hw=(), inc=None, amt=1):
        ws = []
        for evt in hw:
            if evt is None:
                continue
            q, v = evt
            if v <= 0 or waited[eng].get(q, 0) >= v:
                continue
            waited[eng][q] = v
            ws.append((q, v))
        for evt in waits:
            if evt is None:
                continue
            q, v = evt
            if v <= 0:
                continue
            if q in DMAQ:
                v = cnt.get(q, 0)
            elif selfv[eng].get(q, 0) >= v:
                continue
            if waited[eng].get(q, 0) >= v:
                continue
            waited[eng][q] = v
            ws.append((q, v))
        ev = None
        incs = []
        if inc is not None:
            cnt[inc] = cnt.get(inc, 0) + amt
            if inc not in DMAQ:
                selfv[eng][inc] = cnt[inc]
            incs.append((inc, amt))
            ev = (inc, cnt[inc])
        streams[eng].append((ws, emit, incs))
        return ev

    def dma(queue_eng, out_ap, in_ap, waits=(), inc="ld"):
        return op(queue_eng,
                  lambda e, o=out_ap, i=in_ap: e.dma_start(out=o, in_=i),
                  waits=waits, inc=inc, amt=16)

    def emax(evs):
        best = {}
        for evt in evs:
            if evt is None:
                continue
            q, v = evt
            if best.get(q, -1) < v:
                best[q] = v
        return [(q, v) for q, v in best.items()]

    tr = {
        "xs_free": [EV0, EV0],
        "xsb_free": [EV0, EV0],
        "x1b_free": [EV0, EV0],
        "xf_free": [EV0, EV0],
        "dt_free": [EV0, EV0],
        "sp_free": [EV0, EV0],
        "pb01_free": [EV0, EV0],
        "pb23_free": EV0,
        "dA_free": [EV0] * NDA,
        "u_free": [EV0] * NUR,
        "dtx_ev": [EV0, EV0],
        "dtx_free": [[], []],
        "carry": [[EV0] * NBLK, [EV0] * NBLK],
        "bbc_users": [],
        "zS_free": [EV0] * NBLK,
        "z_ev": [EV0] * NBLK,
        "sq_free": [EV0, EV0],
        "pb67_free": EV0,
        "pb45_free": EV0,
        "dtin_ev": [EV0] * NCH,
        "stats_pe": EV0,
        "dbg_gate": EV0,
        "hin_ev": EV0,
        "inj_free": EV0,
        "injdve_free": EV0,
    }

    stg_wxp = zS[:, 17:27, :].rearrange("p a t -> p (a t)").rearrange(
        "p (blk m) -> p blk m", m=160)
    stg_misc = zS[:, 31, :]
    stg_affn = zS[:, 0:8, :].rearrange("p a t -> p (a t)").rearrange(
        "p (blk m) -> p blk m", m=128)
    stg_bwt = zS[:, 8:16, :].rearrange("p a t -> p (a t)")
    stg_route = zS[:, 16, 0:256].rearrange("p (blk m) -> p blk m", m=8)

    ld_wxp = dma("gpsimd", stg_wxp,
                 wxp_h.rearrange("(blk p) m -> p blk m", p=128), inc="gd")
    ld_alog = dma("gpsimd", stg_misc, alog_h, inc="gd")
    ld_affn = dma("gpsimd", stg_affn,
                  affn_h.rearrange("(blk p) m -> p blk m", p=128), inc="gd")
    ld_bwt = dma("gpsimd", stg_bwt, bwt_h, inc="gd")
    ld_route = dma("gpsimd", stg_route,
                   route_h.rearrange("(blk p) m -> p blk m", p=128), inc="gd")
    ld_dv = dma("gpsimd", dvec_s[:], dvec_h, inc="gd")
    ld_bv = dma("gpsimd", bvec[:], bvec_h, inc="gd")
    ld_sb = dma("gpsimd", selB[:],
                selbc_h[:, 0:2048].rearrange("k (j m) -> k j m", m=128),
                inc="gd")
    ld_sc = dma("gpsimd", selC[:],
                selbc_h[:, 2048:4096].rearrange("k (j m) -> k j m", m=128),
                inc="gd")
    ld_se = dma("gpsimd", seler[:], seler_h, inc="gd")
    ld_o1 = dma("gpsimd", ones1[:], ones1_h, inc="gd")
    ld_m1 = dma("gpsimd", selm[:], selm_h, inc="gd")
    ld_m2 = dma("gpsimd", invm[:], invm_h, inc="gd")

    cast_wxp = op("scalar", lambda e: e.copy(wxpb[:], stg_wxp),
                  waits=[ld_wxp], inc="act")
    ev_negA = op("scalar", lambda e: e.activation(
        negA[:].rearrange("p a n -> p (a n)"), stg_misc, AF.Exp),
        waits=[ld_alog], inc="act")
    ev_negA = op("vector", lambda e: e.tensor_scalar(
        negA[:].rearrange("p a n -> p (a n)"),
        negA[:].rearrange("p a n -> p (a n)"), -1.0, None, ALU.mult),
        waits=[ev_negA], inc="dve")
    ev_oneD = op("scalar", lambda e: e.activation(
        oneD[:], dvec_s[:], AF.Copy, bias=1.0), waits=[ld_dv], inc="act")
    stg_wdt = zS[:, 17:25, :].rearrange("p a t -> p (a t)")
    ld_wdt = dma("gpsimd", stg_wdt, wdt_h, waits=[cast_wxp], inc="gd")
    cast_wdt = op("scalar", lambda e: e.copy(
        wdtb[:].rearrange("p a m -> p (a m)"), stg_wdt),
        waits=[ld_wdt], inc="act")
    ev_init = op("vector", lambda e: e.memset(hcarryA[:], 0.0), inc="dve")
    for _i in range(NDA):
        ev_init0 = op("vector", lambda e, _i=_i: e.memset(
            dA_q[_i][:, :, 0:1].rearrange("p a o -> p (a o)"), 0.0),
            inc="dve")
    ev_init2 = op("vector", lambda e: e.memset(rcarry[:], 0.0), inc="dve")
    ev_init3 = op("vector", lambda e: e.memset(onesk[:], 1.0), inc="dve")
    for b in range(NBLK):
        tr["carry"][0][b] = ev_init

    HLD = {"dtgp": EV0, "bbc": EV0, "cbc": EV0, "inj_free": EV0}

    def emit_sweep_block(c, b):
        r2 = b % 2
        lx = dma("sync", xs_rot[r2][:],
                 x_h[b * 128:(b + 1) * 128, c * TC:(c + 1) * TC],
                 waits=[tr["xs_free"][r2]], inc="ld")
        tr["xs_free"][r2] = op(
            "scalar", lambda e, r2=r2: e.copy(xsb_rot[r2][:], xs_rot[r2][:]),
            waits=[lx, tr["xsb_free"][r2]], inc="act")
        w = [tr["xs_free"][r2], cast_wxp]
        if b == 0:
            w.append(tr["pb23_free"])

        def mm(e, b=b):
            e.matmul(pb[2][:], wxpb[:, b, 0:128], xsb_rot[b % 2][:],
                     start=(b == 0), stop=(b == NBLK - 1))
            return e.matmul(pb[3][0:32, :], wxpb[:, b, 128:160],
                            xsb_rot[b % 2][:],
                            start=(b == 0), stop=(b == NBLK - 1))
        ev = op("tensor", mm, waits=w, inc="pe")
        tr["xsb_free"] = [ev, ev]
        if b == NBLK - 1:
            op("scalar", lambda e, c=c: e.copy(dtin_all[:, c, :], pb[2][:]),
               waits=[ev], inc="act")
            ev2 = op("scalar", lambda e, c=c: e.copy(bc_all[:, c, :],
                                                     pb[3][0:32, :]),
                     inc="act")
            tr["dtin_ev"][c] = ev2
            tr["pb23_free"] = ev2

    def emit_bcast(c, with_C):
        users = emax(tr["bbc_users"])
        tr["bbc_users"] = []
        nj = 32 if with_C else 16
        last = EV0
        for j in range(nj):
            bank = 2 + j % 2
            w = [tr["dtin_ev"][c]]
            if j < 2:
                w.append(tr["pb23_free"])
            if last is not None:
                w.append(last)

            def mm(e, j=j, bank=bank, c=c):
                sel = selC[:, j - 16, :] if j >= 16 else selB[:, j, :]
                return e.matmul(pb[bank][:], sel, bc_all[:, c, :],
                                start=True, stop=True)
            pe = op("tensor", mm, waits=w + ([ld_sb, ld_sc] if j == 0 else []),
                    inc="pe")

            def ev(e, j=j, bank=bank):
                dst = C_bc[:, j - 16, :] if j >= 16 else B_bc[:, j, :]
                return e.copy(dst, pb[bank][:])
            last = op("scalar", ev, waits=[pe] + (users if j == 0 else []),
                      inc="act")
        tr["pb23_free"] = last
        return last

    def emit_front(c, b, phase):
        r2 = b % 2
        lx = dma("sync", xf_rot[r2][:],
                 x_h[b * 128:(b + 1) * 128, c * TC:(c + 1) * TC],
                 waits=[tr["xf_free"][r2]], inc="ld")

        def mm_dt(e, c=c, b=b, r2=r2):
            return e.matmul(pb[r2][:], wdtb[:, b, :], dtin_all[:, c, :],
                            start=True, stop=True)
        dtmm = op("tensor", mm_dt,
                  waits=[tr["dtin_ev"][c], tr["pb01_free"][r2], cast_wdt],
                  inc="pe")

        def sp1f(e, b=b, r2=r2):
            e.activation(sp1_rot[r2][:], pb[r2][:], AF.Abs,
                         bias=bvec[:, b:b + 1])
            return e.activation(dt_rot[r2][:], pb[r2][:], AF.Relu,
                                bias=bvec[:, b:b + 1],
                                accum_out=rs_acc[:, r2:r2 + 1])
        spa = op("scalar", sp1f,
                 waits=[dtmm, tr["dt_free"][r2], tr["sp_free"][r2], ld_bv],
                 inc="act")
        spb = op("scalar", lambda e, r2=r2: e.activation(
            pb[r2][:], sp1_rot[r2][:], AF.Exp, scale=-1.0),
            hw=[spa], inc="act")
        spv = op("scalar", lambda e, r2=r2: e.activation(
            sp1_rot[r2][:], pb[r2][:], AF.Ln, bias=1.0,
            accum_out=rs_acc[:, 2 + r2:3 + r2]),
            hw=[spb], inc="act")
        tr["pb01_free"][r2] = spv

        def gadd(e, b=b, r2=r2, phase=phase):
            i = e.tensor_tensor(dt_rot[r2][:], dt_rot[r2][:], sp1_rot[r2][:],
                                ALU.add)
            if phase == "A":
                e.tensor_tensor(rs_acc[:, r2:r2 + 1], rs_acc[:, r2:r2 + 1],
                                rs_acc[:, 2 + r2:3 + r2], ALU.add)
                i = e.tensor_tensor(rcarry[:, b:b + 1], rcarry[:, b:b + 1],
                                    rs_acc[:, r2:r2 + 1], ALU.add)
            return i
        gv = op("gpsimd", gadd, waits=[spv, ev_init2], inc="gp")
        tr["sp_free"][r2] = gv
        dtxv = op("gpsimd",
                  lambda e, r2=r2: e.tensor_tensor(
                      dtx_rot[r2][:], dt_rot[r2][:], xf_rot[r2][:], ALU.mult),
                  waits=[lx] + tr["dtx_free"][r2], inc="gp")
        tr["dt_free"][r2] = dtxv
        tr["dtx_ev"][r2] = dtxv
        if phase == "A":
            tr["xf_free"][r2] = dtxv
        HLD["dtgp"] = gv
        return gv

    def emit_scanwork(c, b, phase, gctr):
        r2 = b % 2
        carry_t = hcarryA if phase == "A" else Hin
        pidx = 0 if phase == "A" else 1
        carry_ev = tr["carry"][pidx][b]

        dA0v = op("scalar", lambda e, b=b, r2=r2: e.activation(
            inj[:, 0:16], negA[:, b, :], AF.Exp,
            scale=dt_rot[r2][:, 0:1]),
            waits=[HLD["dtgp"], ev_negA, tr["inj_free"]], inc="act")
        injA = op("gpsimd", lambda e, b=b, carry_t=carry_t: e.tensor_tensor(
            inj[:, 16:32], inj[:, 0:16], carry_t[:, b, :], ALU.mult),
            waits=[dA0v, carry_ev, tr["injdve_free"]], inc="gp")

        last_gp_u = EV0
        last_dve_u = EV0
        sev = EV0
        for q in range(QN):
            g = gctr[0]
            gctr[0] += 1
            s = g % NDA
            su = g % NUR

            def dA_emit(e, b=b, q=q, s=s, r2=r2):
                e.activation(dA_q[s][:, 0, 1:TC], dt_rot[r2][:, 1:TC],
                             AF.Exp, scale=negA[:, b, 2 * q:2 * q + 1])
                return e.activation(dA_q[s][:, 1, 1:TC], dt_rot[r2][:, 1:TC],
                                    AF.Exp,
                                    scale=negA[:, b, 2 * q + 1:2 * q + 2])
            dAev = op("scalar", dA_emit,
                      waits=[HLD["dtgp"], tr["dA_free"][s], ev_negA, ev_init0],
                      inc="act")

            on_dve = (phase == "A" and q >= QN - DVEQ_A)
            eng = "vector" if on_dve else "gpsimd"
            qn = "dve" if on_dve else "gp"

            def u_emit(e, q=q, su=su, r2=r2):
                e.tensor_tensor(u_q[su][:, 0, :], dtx_rot[r2][:],
                                B_bc[:, 2 * q, :], ALU.mult)
                return e.tensor_tensor(u_q[su][:, 1, :], dtx_rot[r2][:],
                                       B_bc[:, 2 * q + 1, :], ALU.mult)
            uev = op(eng, u_emit,
                     waits=[tr["dtx_ev"][r2], tr["u_free"][su], HLD["bbc"]],
                     inc=qn)

            def fix_emit(e, q=q, su=su):
                u0 = u_q[su][:, :, 0:1].rearrange("p a o -> p (a o)")
                return e.tensor_tensor(u0, u0, inj[:, 16 + 2 * q:18 + 2 * q],
                                       ALU.add)
            fev = op(eng, fix_emit, waits=[injA],
                     hw=[uev] if on_dve else (), inc=qn)
            if on_dve:
                last_dve_u = fev
            else:
                last_gp_u = fev

            def scan_emit(e, q=q, s=s, su=su):
                return e.tensor_tensor_scan(
                    p_bf[:, 2 * q:2 * q + 2, :].rearrange("p a t -> p (a t)"),
                    dA_q[s][:].rearrange("p a t -> p (a t)"),
                    u_q[su][:].rearrange("p a t -> p (a t)"),
                    0.0, ALU.mult, ALU.add)
            sev = op("vector", scan_emit,
                     waits=[uev, fev, dAev], inc="dve")
            tr["dA_free"][s] = sev
            tr["u_free"][su] = sev
        tr["inj_free"] = sev
        if last_dve_u is not None:
            tr["injdve_free"] = last_dve_u
        tr["dtx_free"][r2] = [ev for ev in (last_gp_u, last_dve_u)
                              if ev is not None]
        tr["bbc_users"].append(sev)

        cev = op("vector", lambda e, b=b, carry_t=carry_t: e.tensor_copy(
            carry_t[:, b, :],
            p_bf[:, :, TC - 1:TC].rearrange("p a o -> p (a o)")),
            waits=[tr["dbg_gate"]], hw=[sev], inc="dve")
        tr["carry"][pidx][b] = cev

        if phase == "B":
            def cm(e):
                e.tensor_tensor(p_bf[:, 0:8, :].rearrange("p a t -> p (a t)"),
                                p_bf[:, 0:8, :].rearrange("p a t -> p (a t)"),
                                C_bc[:, 0:8, :].rearrange("p a t -> p (a t)"),
                                ALU.mult)
                return e.tensor_tensor(
                    p_bf[:, 8:16, :].rearrange("p a t -> p (a t)"),
                    p_bf[:, 8:16, :].rearrange("p a t -> p (a t)"),
                    C_bc[:, 8:16, :].rearrange("p a t -> p (a t)"), ALU.mult)
            cmev = op("vector", cm, waits=[HLD["cbc"]], hw=[cev], inc="dve")
            tr["bbc_users"].append(cmev)
            t1 = op("vector", lambda e: e.tensor_tensor(
                p_bf[:, 0:8, :].rearrange("p a t -> p (a t)"),
                p_bf[:, 0:8, :].rearrange("p a t -> p (a t)"),
                p_bf[:, 8:16, :].rearrange("p a t -> p (a t)"), ALU.add),
                hw=[cmev], inc="dve")
            t2 = op("vector", lambda e: e.tensor_tensor(
                p_bf[:, 0:4, :].rearrange("p a t -> p (a t)"),
                p_bf[:, 0:4, :].rearrange("p a t -> p (a t)"),
                p_bf[:, 4:8, :].rearrange("p a t -> p (a t)"), ALU.add),
                hw=[t1], inc="dve")
            t3 = op("vector", lambda e: e.tensor_tensor(
                p_bf[:, 0:2, :].rearrange("p a t -> p (a t)"),
                p_bf[:, 0:2, :].rearrange("p a t -> p (a t)"),
                p_bf[:, 2:4, :].rearrange("p a t -> p (a t)"), ALU.add),
                hw=[t2], inc="dve")
            t4 = op("vector", lambda e, r2=r2: e.tensor_tensor(
                sq_rot[r2][:], p_bf[:, 0, :], p_bf[:, 1, :], ALU.add),
                waits=[tr["sq_free"][r2]], hw=[t3], inc="dve")
            zev = op("vector", lambda e, b=b, r2=r2: e.scalar_tensor_tensor(
                zS[:, b, :], xf_rot[r2][:], oneD[:, b:b + 1], sq_rot[r2][:],
                ALU.mult, ALU.add),
                waits=[ev_oneD, tr["zS_free"][b]], hw=[t4], inc="dve")
            tr["z_ev"][b] = zev
            tr["xf_free"][r2] = zev
            sqev = op("scalar", lambda e, b=b, r2=r2: e.activation(
                sq_rot[r2][:], zS[:, b, :], AF.Square),
                waits=[zev, tr["sq_free"][r2]], inc="act")

            def smm(e, b=b, r2=r2):
                e.matmul(pb[6][0:1, :], onesk[:], zS[:, b, :],
                         start=(b == 0), stop=(b == NBLK - 1))
                return e.matmul(pb[7][0:1, :], onesk[:], sq_rot[r2][:],
                                start=(b == 0), stop=(b == NBLK - 1))
            w = [sqev, ev_init3]
            if b == 0:
                w.append(tr["pb67_free"])
            pev = op("tensor", smm, waits=w, inc="pe")
            tr["sq_free"] = [pev, pev]
            if b == NBLK - 1:
                tr["stats_pe"] = pev

    def emit_stats_math():
        m1 = op("vector", lambda e: e.tensor_scalar(
            st_m, pb[6][0:1, :], 1.0 / DM, None, ALU.mult),
            waits=[tr["stats_pe"]], inc="dve")
        m2 = op("vector", lambda e: e.tensor_scalar(
            st_v, pb[7][0:1, :], 1.0 / DM, None, ALU.mult),
            hw=[m1], inc="dve")
        m3 = op("vector", lambda e: e.tensor_tensor(
            st_sc, st_m, st_m, ALU.mult), hw=[m2], inc="dve")
        m4 = op("vector", lambda e: e.tensor_tensor(
            st_v, st_v, st_sc, ALU.subtract), hw=[m3], inc="dve")
        m5 = op("vector", lambda e: e.tensor_scalar(
            st_v, st_v, EPS, None, ALU.add), hw=[m4], inc="dve")
        r1 = op("scalar", lambda e: e.activation(st_v, st_v, AF.Ln),
                waits=[m5], inc="act")
        r2 = op("scalar", lambda e: e.activation(st_v, st_v, AF.Exp,
                                                 scale=-0.5),
                hw=[r1], inc="act")
        mr = op("vector", lambda e: e.tensor_tensor(
            st_m, st_m, st_v, ALU.mult), waits=[r2], inc="dve")

        def bc(e):
            e.matmul(pb[4][:], ones1[:], st_v, start=True, stop=True)
            return e.matmul(pb[5][:], ones1[:], st_m, start=True, stop=True)
        bcp = op("tensor", bc, waits=[mr, tr["pb45_free"], ld_o1], inc="pe")
        return mr, bcp

    for b in range(NBLK):
        emit_sweep_block(0, b)
    if dbg:
        dma("gpsimd", dbgODm_h[0], oneD[:], waits=[tr["dtin_ev"][0]],
            inc="gd")
    for c in range(NCH):
        HLD["bbc"] = emit_bcast(c, with_C=False)
        gctr = [0]
        for b in range(NBLK):
            emit_front(c, b, "A")
            emit_scanwork(c, b, "A", gctr)
            if c + 1 < NCH:
                emit_sweep_block(c + 1, b)
        if dbg:
            if c == 0:
                dma("gpsimd", dbgODm_h[1], oneD[:],
                    waits=emax(tr["carry"][0]), inc="gd")
            if c == NCH - 1:
                dma("gpsimd", dbgODm_h[2], oneD[:],
                    waits=emax(tr["carry"][0]), inc="gd")
            dgev = dma("gpsimd", dbgSc_h[c],
                       hcarryA[:].rearrange("p a b -> p (a b)"),
                       waits=emax(tr["carry"][0]), inc="gd")
            tr["dbg_gate"] = dgev
            if c == 0:
                dma("gpsimd", dbgDT_h, dt_rot[1][:], waits=[dgev], inc="gd")
                dma("gpsimd", dbgDTX_h, dtx_rot[1][:], inc="gd")
                dma("gpsimd", dbgB_h,
                    B_bc[:].rearrange("p a t -> p (a t)"), inc="gd")
                dgev = dma("gpsimd", dbgP_h,
                           p_bf[:].rearrange("p a t -> p (a t)"), inc="gd")
                tr["dbg_gate"] = dgev

    pc_ev = EV0
    for b in range(NBLK):
        pc_ev = op("scalar", lambda e, b=b: e.activation(
            Pc[:, b, :], negA[:, b, :], AF.Exp, scale=rcarry[:, b:b + 1]),
            waits=([tr["sp_free"][0], tr["sp_free"][1]] if b == 0 else []),
            inc="act")
    cc1 = dma("gpsimd", cc_in[0:DM, :].rearrange("(blk p) n -> p blk n", p=128),
              hcarryA[:],
              waits=emax(tr["carry"][0] + [tr["dbg_gate"]]), inc="cc")
    cc2 = dma("gpsimd",
              cc_in[DM:2 * DM, :].rearrange("(blk p) n -> p blk n", p=128),
              Pc, waits=[pc_ev], inc="cc")
    ccv = op("gpsimd", lambda e: e.collective_compute(
        "AllGather", ALU.bypass, replica_groups=[list(range(NCORES))],
        ins=[cc_in.ap().opt()], outs=[cc_out.ap().opt()]),
        waits=[cc1, cc2], inc="cc")
    ccf = dma("gpsimd", ccf_in.ap(), cc_in[0:1, :], waits=[ccv], inc="cc")
    ccv = op("gpsimd", lambda e: e.collective_compute(
        "AllReduce", ALU.add, replica_groups=[list(range(NCORES))],
        ins=[ccf_in.ap().opt()], outs=[ccf_out.ap().opt()]),
        waits=[ccf], inc="cc")
    if dbg:
        dma("gpsimd", dbgS_h, hcarryA[:].rearrange("p a b -> p (a b)"),
            waits=[ccv], inc="gd")

    esW1.close()
    esW2 = contextlib.ExitStack()
    EW2 = esW2.enter_context
    gS1 = EW2(nc.sbuf_tensor("gS1", [128, NBLK * NST], F32, side="right"))
    gP1 = EW2(nc.sbuf_tensor("gP1", [128, NBLK * NST], F32, side="right"))
    pprime = EW2(nc.sbuf_tensor("pprime", [128, NBLK * NST], F32, side="right"))

    hv = op("vector", lambda e: e.memset(Hin[:], 0.0), waits=[ccv],
            inc="dve")
    Hf = Hin[:].rearrange("p a b -> p (a b)")
    for j in range(NCORES):
        base = j * 2 * DM
        gjs = dma("gpsimd",
                  gS1[:].rearrange("p (blk n) -> p blk n", n=NST),
                  cc_out[base:base + DM, :].rearrange(
                      "(blk p) n -> p blk n", p=128),
                  waits=[ccv, hv], inc="gd")
        gjp = dma("gpsimd",
                  gP1[:].rearrange("p (blk n) -> p blk n", n=NST),
                  cc_out[base + DM:base + 2 * DM, :].rearrange(
                      "(blk p) n -> p blk n", p=128), inc="gd")
        hv = op("vector", lambda e, j=j: e.tensor_scalar(
            pprime[:], gP1[:], selm[:, j:j + 1], invm[:, j:j + 1],
            ALU.mult, ALU.add), waits=[ld_m1, ld_m2, gjs, gjp], hw=[hv],
            inc="dve")
        hv = op("vector", lambda e: e.tensor_tensor(Hf, Hf, pprime[:],
                                                    ALU.mult),
                hw=[hv], inc="dve")
        hv = op("vector", lambda e, j=j: e.tensor_scalar(
            pprime[:], gS1[:], selm[:, j:j + 1], None, ALU.mult),
            hw=[hv], inc="dve")
        hv = op("vector", lambda e: e.tensor_tensor(Hf, Hf, pprime[:],
                                                    ALU.add),
                hw=[hv], inc="dve")
    hv = op("vector", lambda e: e.nop(), hw=[hv], inc="dve")
    tr["hin_ev"] = hv
    if dbg:
        dma("gpsimd", dbgH_h, Hin[:].rearrange("p a b -> p (a b)"),
            waits=[hv], inc="gd")
        dma("gpsimd", dbgODm_h[3], oneD[:], waits=[hv], inc="gd")

    esW2.close()
    esW3 = contextlib.ExitStack()
    EW3 = esW3.enter_context
    affnb = EW3(nc.sbuf_tensor("affnb", [128, NBLK, 128], BF16, side="right"))
    bwtb = EW3(nc.sbuf_tensor("bwtb", [128, NBLK, 128], BF16, side="right"))
    wc1 = op("scalar", lambda e: e.copy(affnb[:], stg_affn),
             waits=[ld_affn, hv], inc="act")
    wc2 = op("scalar", lambda e: e.copy(
        bwtb[:].rearrange("p a m -> p (a m)"), stg_bwt),
        waits=[ld_bwt, hv], inc="act")
    wc3 = op("scalar", lambda e: e.copy(routeb[:], stg_route),
             waits=[ld_route], inc="act")
    for b in range(NBLK):
        tr["zS_free"][b] = wc3
        tr["carry"][1][b] = hv

    for _e in ("tensor", "scalar", "vector", "gpsimd"):
        op(_e, lambda e: e.nop(), waits=[hv, ccv])

    for c in range(NCH):
        cb = emit_bcast(c, with_C=True)
        HLD["bbc"] = cb
        HLD["cbc"] = cb
        gctr = [0]
        for b in range(NBLK):
            emit_front(c, b, "B")
            emit_scanwork(c, b, "B", gctr)

        dbgz_ev = EV0
        if dbg and c == 0:
            dbgz_ev = dma("gpsimd", dbgZ_h,
                          zS[:].rearrange("p a t -> p (a t)"),
                          waits=emax(tr["z_ev"]), inc="gd")
        mr1, bc1 = emit_stats_math()
        ln1 = dbgz_ev
        ffn_pe = EV0
        for b in range(NBLK):
            ln1 = op("vector", lambda e, b=b: e.tensor_tensor(
                zS[:, b, :], zS[:, b, :], pb[4][:], ALU.mult),
                waits=[bc1, tr["z_ev"][b]], hw=([ln1] if ln1 else ()),
                inc="dve")
            ln1 = op("vector", lambda e, b=b: e.tensor_tensor(
                zS[:, b, :], zS[:, b, :], pb[5][:], ALU.subtract),
                hw=[ln1], inc="dve")
            xc = op("scalar", lambda e, b=b: e.copy(x1b_rot[b % 2][:],
                                                    zS[:, b, :]),
                    waits=[ln1, tr["x1b_free"][b % 2]], inc="act")

            def fmm(e, b=b):
                e.matmul(pb[6][0:8, :], routeb[:, b, :], x1b_rot[b % 2][:],
                         start=(b == 0), stop=(b == NBLK - 1))
                return e.matmul(pb[7][:], affnb[:, b, :], x1b_rot[b % 2][:],
                                start=(b == 0), stop=(b == NBLK - 1))
            ffn_pe = op("tensor", fmm, waits=[xc, wc1, wc2, wc3], inc="pe")
            tr["x1b_free"][b % 2] = ffn_pe
        tr["pb45_free"] = ln1

        dbgx_ev = EV0
        if dbg and c == 0:
            dbgx_ev = dma("gpsimd", dbgX1_h,
                          zS[:].rearrange("p a t -> p (a t)"),
                          waits=[ln1], inc="gd")
        lge = op("scalar", lambda e: e.copy(lg_s[:], pb[6][0:8, :]),
                 waits=[ffn_pe], inc="act")
        if dbg and c == 0:
            lge = dma("gpsimd", dbgLG_h, lg_s[:], waits=[lge], inc="gd")
        s1 = dma("sync", tmp8[0:4, :], lg_s[4:8, :], waits=[lge], inc="sd")
        v1 = op("vector", lambda e: e.tensor_tensor(
            tmp8[0:4, :], lg_s[0:4, :], tmp8[0:4, :], ALU.max),
            waits=[s1], inc="dve")
        s2 = dma("sync", hf_s[0:2, :], tmp8[2:4, :], waits=[v1], inc="sd")
        v2 = op("vector", lambda e: e.tensor_tensor(
            tmp8[0:2, :], tmp8[0:2, :], hf_s[0:2, :], ALU.max),
            waits=[s2], inc="dve")
        s3 = dma("sync", hf_s[0:1, :], tmp8[1:2, :], waits=[v2], inc="sd")
        v3 = op("vector", lambda e: e.tensor_tensor(
            tmp8[0:1, :], tmp8[0:1, :], hf_s[0:1, :], ALU.max),
            waits=[s3], inc="dve")
        m8 = op("tensor", lambda e: e.matmul(
            pb[4][0:8, :], ones1[:, 0:8], tmp8[0:1, :], start=True, stop=True),
            waits=[v3], inc="pe")
        v4 = op("vector", lambda e: e.tensor_tensor(
            tmp8[:], lg_s[:], pb[4][0:8, :], ALU.is_equal),
            waits=[m8], inc="dve")
        v5 = op("vector", lambda e: e.tensor_tensor(
            tmp8[:], tmp8[:], lg_s[:], ALU.mult), hw=[v4], inc="dve")
        if dbg and c == 0:
            v5 = dma("gpsimd", dbgRW_h, tmp8[:], waits=[v5], inc="gd")
        mkp = op("tensor", lambda e: e.matmul(
            pb[5][:], seler[:], tmp8[:], start=True, stop=True),
            waits=[v5, ld_se], inc="pe")
        hfe = op("scalar", lambda e: e.copy(hf_s[:], pb[7][:]),
                 waits=[mkp], inc="act")
        whv = op("vector", lambda e: e.tensor_tensor(
            hf_s[:], hf_s[:], pb[5][:], ALU.mult), waits=[hfe], inc="dve")
        if dbg and c == 0:
            whv = dma("gpsimd", dbgWH_h, hf_s[:], waits=[whv], inc="gd")
        whc = op("scalar", lambda e: e.copy(whb[:], hf_s[:]),
                 waits=[whv], inc="act")
        if dbg and c == 0:
            z2p = [whv, whv]

        z2p = [whv, whv]
        sq2e = [EV0, EV0]
        for b in range(NBLK):
            bank = 4 + b % 2

            def fo(e, b=b, bank=bank):
                return e.matmul(pb[bank][:], bwtb[:, b, :], whb[:],
                                start=True, stop=True)
            fop = op("tensor", fo, waits=[whc, z2p[b % 2]], inc="pe")
            z2 = op("vector", lambda e, b=b, bank=bank: e.tensor_tensor(
                zS[:, b, :], zS[:, b, :], pb[bank][:], ALU.add),
                waits=[fop], inc="dve")
            z2p[b % 2] = z2
            sq2 = op("scalar", lambda e, b=b: e.activation(
                sq_rot[b % 2][:], zS[:, b, :], AF.Square),
                waits=[z2, sq2e[b % 2]], inc="act")

            def smm2(e, b=b):
                e.matmul(pb[6][0:1, :], onesk[:], zS[:, b, :],
                         start=(b == 0), stop=(b == NBLK - 1))
                return e.matmul(pb[7][0:1, :], onesk[:], sq_rot[b % 2][:],
                                start=(b == 0), stop=(b == NBLK - 1))
            st2 = op("tensor", smm2, waits=[sq2], inc="pe")
            sq2e = [st2, st2]
            if b == NBLK - 1:
                tr["stats_pe"] = st2
        tr["sq_free"] = [tr["stats_pe"], tr["stats_pe"]]

        dbgz2_ev = EV0
        if dbg and c == 0:
            dbgz2_ev = dma("gpsimd", dbgZ2_h,
                           zS[:].rearrange("p a t -> p (a t)"),
                           waits=emax(z2p), inc="gd")
        mr2, bc2 = emit_stats_math()
        tr["pb67_free"] = mr2
        ln2 = dbgz2_ev
        for b in range(NBLK):
            ln2 = op("vector", lambda e, b=b: e.tensor_tensor(
                zS[:, b, :], zS[:, b, :], pb[4][:], ALU.mult),
                waits=[bc2], hw=([ln2] if ln2 else ()), inc="dve")
            ln2 = op("vector", lambda e, b=b: e.tensor_tensor(
                zS[:, b, :], zS[:, b, :], pb[5][:], ALU.subtract),
                hw=[ln2], inc="dve")
            stv = dma("sync",
                      out_h[b * 128:(b + 1) * 128, c * TC:(c + 1) * TC],
                      zS[:, b, :], waits=[ln2], inc="st")
            tr["zS_free"][b] = stv
        tr["pb45_free"] = ln2
        bwaits = [(q, cnt.get(q, 0)) for q in ("pe", "act", "dve", "gp")]
        for _e in ("tensor", "scalar", "vector", "gpsimd"):
            op(_e, lambda e: e.nop(), waits=list(bwaits))
        if dbg and c == 0:
            dma("gpsimd", dbgODm_h[4], oneD[:], waits=[ln2], inc="gd")

    if dbg:
        dma("gpsimd", dbgOD_h, oneD[:],
            waits=[("st", cnt.get("st", 0))], inc="gd")
        dma("gpsimd", dbgDV_h, dvec_s[:], inc="gd")

    op("gpsimd", lambda e: e.nop(),
       waits=[("st", cnt.get("st", 0)), ("gd", cnt.get("gd", 0)),
              ("ld", cnt.get("ld", 0)), ("sd", cnt.get("sd", 0)),
              ("cc", cnt.get("cc", 0))])

    sems = {q: E(nc.semaphore("q_" + q)) for q in sorted(cnt)}
    blk = E(nc.Block())
    for eng in ENG:
        sec = getattr(blk, eng)

        def body(e, eng=eng):
            for (ws, emit, incs) in streams[eng]:
                for (q, v) in ws:
                    e.wait_ge(sems[q], v)
                inst = emit(e)
                for (q, amt) in incs:
                    inst.then_inc(sems[q], amt)
        sec(body)

    esW3.close()
    es.close()
    return nc


def _host_prep(L0_attn_x, L0_attn_A_log, L0_attn_D, L0_attn_x_proj_w,
               L0_attn_dt_proj_w, L0_attn_dt_proj_b, L0_ffn_route_w, L0_ffn_A,
               L0_ffn_Bw):
    import ml_dtypes
    x = np.asarray(L0_attn_x, dtype=np.float32)
    A_log = np.asarray(L0_attn_A_log, dtype=np.float32)
    Dv = np.asarray(L0_attn_D, dtype=np.float32)
    xp_w = np.asarray(L0_attn_x_proj_w, dtype=np.float32)
    dt_w = np.asarray(L0_attn_dt_proj_w, dtype=np.float32)
    dt_b = np.asarray(L0_attn_dt_proj_b, dtype=np.float32)
    route_w = np.asarray(L0_ffn_route_w, dtype=np.float32)
    A_ffn = np.asarray(L0_ffn_A, dtype=np.float32)
    Bw = np.asarray(L0_ffn_Bw, dtype=np.float32)

    wxp = np.ascontiguousarray(xp_w.T)
    wdt = np.ascontiguousarray(dt_w.T)
    alog = np.ascontiguousarray(
        A_log.reshape(NBLK, 128, NST).transpose(1, 0, 2)).reshape(
        128, NBLK * NST)
    dvec = np.ascontiguousarray(Dv.reshape(NBLK, 128).T)
    bvec = np.ascontiguousarray(dt_b.reshape(NBLK, 128).T)
    route = np.ascontiguousarray(route_w.T)
    affn = np.ascontiguousarray(np.transpose(A_ffn, (2, 0, 1)).reshape(DM, 128))
    bwt = np.ascontiguousarray(np.transpose(Bw, (0, 2, 1)).reshape(128, DM))
    selbc = np.zeros((32, 32 * 128), dtype=ml_dtypes.bfloat16)
    for j in range(32):
        selbc[j, j * 128:(j + 1) * 128] = 1.0
    seler = np.zeros((8, 128), dtype=np.float32)
    for e in range(8):
        seler[e, e * 16:(e + 1) * 16] = 1.0
    ones1 = np.ones((1, 128), dtype=np.float32)

    in_maps = []
    for cid in range(NCORES):
        bb, qq = cid // 4, cid % 4
        xc = np.ascontiguousarray(x[bb, qq * TLOC:(qq + 1) * TLOC, :].T)
        sel = np.zeros(8, dtype=np.float32)
        for j in range(NCORES):
            if j // 4 == bb and j % 4 < qq:
                sel[j] = 1.0
        in_maps.append({
            "x": xc, "wxp": wxp, "wdt": wdt, "alog": alog, "dvec": dvec,
            "bvec": bvec, "route": route, "affn": affn, "bwt": bwt,
            "selbc": selbc, "seler": seler, "ones1": ones1,
            "selm": np.ascontiguousarray(np.tile(sel, (128, 1))),
            "invm": np.ascontiguousarray(np.tile(1.0 - sel, (128, 1))),
        })
    return in_maps


def kernel(L0_attn_x, L0_attn_A_log, L0_attn_D, L0_attn_x_proj_w,
           L0_attn_dt_proj_w, L0_attn_dt_proj_b, L0_ffn_route_w, L0_ffn_A,
           L0_ffn_Bw):
    in_maps = _host_prep(L0_attn_x, L0_attn_A_log, L0_attn_D, L0_attn_x_proj_w,
                         L0_attn_dt_proj_w, L0_attn_dt_proj_b, L0_ffn_route_w,
                         L0_ffn_A, L0_ffn_Bw)
    nc = build_nc()
    res = run_bass_kernel_spmd(nc, in_maps, core_ids=list(range(NCORES)))
    out = np.empty((2, 8192, DM), dtype=np.float32)
    for cid in range(NCORES):
        bb, qq = cid // 4, cid % 4
        out[bb, qq * TLOC:(qq + 1) * TLOC, :] = res.results[cid]["o"].T
    return out


# version 10
# speedup vs baseline: 1.2792x; 1.0339x over previous
import contextlib
import numpy as np
import concourse.bass as bass
import concourse.mybir as mybir
from concourse.bass_utils import run_bass_kernel_spmd

F32 = mybir.dt.float32
BF16 = mybir.dt.bfloat16
AF = mybir.ActivationFunctionType
ALU = mybir.AluOpType

NCORES = 8
DM = 4096
NBLK = 32
TLOC = 2048
TC = 512
NCH = TLOC // TC
NST = 16
EPS = 1e-5
NQ = 2
QN = NST // NQ
NDA = 3
NUR = 2
DVEQ_A = 1

DMAQ = ("ld", "st", "sd", "gd", "cc")


def build_nc(dbg=False):
    nc = bass.Bass("TRN2", target_bir_lowering=False, debug=False)

    x_h = nc.dram_tensor("x", [DM, TLOC], F32, kind="ExternalInput").ap()
    wxp_h = nc.dram_tensor("wxp", [DM, 160], F32, kind="ExternalInput").ap()
    wdt_h = nc.dram_tensor("wdt", [128, DM], F32, kind="ExternalInput").ap()
    alog_h = nc.dram_tensor("alog", [128, NBLK * NST], F32, kind="ExternalInput").ap()
    dvec_h = nc.dram_tensor("dvec", [128, NBLK], F32, kind="ExternalInput").ap()
    bvec_h = nc.dram_tensor("bvec", [128, NBLK], F32, kind="ExternalInput").ap()
    route_h = nc.dram_tensor("route", [DM, 8], F32, kind="ExternalInput").ap()
    affn_h = nc.dram_tensor("affn", [DM, 128], F32, kind="ExternalInput").ap()
    bwt_h = nc.dram_tensor("bwt", [128, DM], F32, kind="ExternalInput").ap()
    selbc_h = nc.dram_tensor("selbc", [32, 32 * 128], BF16, kind="ExternalInput").ap()
    seler_h = nc.dram_tensor("seler", [8, 128], F32, kind="ExternalInput").ap()
    ones1_h = nc.dram_tensor("ones1", [1, 128], F32, kind="ExternalInput").ap()
    selm_h = nc.dram_tensor("selm", [128, 8], F32, kind="ExternalInput").ap()
    invm_h = nc.dram_tensor("invm", [128, 8], F32, kind="ExternalInput").ap()
    out_h = nc.dram_tensor("o", [DM, TLOC], F32, kind="ExternalOutput").ap()

    cc_in = nc.dram_tensor("cc_in", [2 * DM, NST], F32)
    cc_out = nc.dram_tensor("cc_out", [2 * DM * NCORES, NST], F32)
    ccf_in = nc.dram_tensor("ccf_in", [1, NST], F32)
    ccf_out = nc.dram_tensor("ccf_out", [1, NST], F32)
    if dbg:
        dbgS_h = nc.dram_tensor("dbg_S", [128, NBLK * NST], F32,
                                kind="ExternalOutput").ap()
        dbgSc_h = [nc.dram_tensor(f"dbg_Sc{i}", [128, NBLK * NST], F32,
                                  kind="ExternalOutput").ap() for i in range(NCH)]
        dbgH_h = nc.dram_tensor("dbg_H", [128, NBLK * NST], F32,
                                kind="ExternalOutput").ap()
        dbgDT_h = nc.dram_tensor("dbg_dt", [128, TC], F32,
                                 kind="ExternalOutput").ap()
        dbgDTX_h = nc.dram_tensor("dbg_dtx", [128, TC], BF16,
                                  kind="ExternalOutput").ap()
        dbgB_h = nc.dram_tensor("dbg_B", [128, NST * TC], BF16,
                                kind="ExternalOutput").ap()
        dbgP_h = nc.dram_tensor("dbg_p", [128, NST * TC], BF16,
                                kind="ExternalOutput").ap()
        dbgZ_h = nc.dram_tensor("dbg_z", [128, NBLK * TC], F32,
                                kind="ExternalOutput").ap()
        dbgX1_h = nc.dram_tensor("dbg_x1", [128, NBLK * TC], F32,
                                 kind="ExternalOutput").ap()
        dbgLG_h = nc.dram_tensor("dbg_lg", [8, TC], F32,
                                 kind="ExternalOutput").ap()
        dbgRW_h = nc.dram_tensor("dbg_rw", [8, TC], F32,
                                 kind="ExternalOutput").ap()
        dbgWH_h = nc.dram_tensor("dbg_wh", [128, TC], F32,
                                 kind="ExternalOutput").ap()
        dbgZ2_h = nc.dram_tensor("dbg_z2", [128, NBLK * TC], F32,
                                 kind="ExternalOutput").ap()
        dbgOD_h = nc.dram_tensor("dbg_od", [128, NBLK], F32,
                                 kind="ExternalOutput").ap()
        dbgODm_h = [nc.dram_tensor(f"dbg_od{i}", [128, NBLK], F32,
                                   kind="ExternalOutput").ap()
                    for i in range(5)]
        dbgDV_h = nc.dram_tensor("dbg_dv", [128, NBLK], F32,
                                 kind="ExternalOutput").ap()

    es = contextlib.ExitStack()
    E = es.enter_context

    zS = E(nc.sbuf_tensor("zS", [128, NBLK, TC], F32))
    p_bf = E(nc.sbuf_tensor("p_bf", [128, NST, TC], BF16))
    B_bc = E(nc.sbuf_tensor("B_bc", [128, NST, TC], BF16))
    C_bc = E(nc.sbuf_tensor("C_bc", [128, NST, TC], BF16))
    dA_q = [E(nc.sbuf_tensor(f"dA{i}", [128, NQ, TC], F32)) for i in range(NDA)]
    u_q = [E(nc.sbuf_tensor(f"u{i}", [128, NQ, TC], F32)) for i in range(NUR)]
    dtin_all = E(nc.sbuf_tensor("dtin_all", [128, NCH, TC], BF16))
    bc_all = E(nc.sbuf_tensor("bc_all", [32, NCH, TC], BF16))
    wdtb = E(nc.sbuf_tensor("wdtb", [128, NBLK, 128], BF16))
    selB = E(nc.sbuf_tensor("selB_s", [32, 16, 128], BF16))
    selC = E(nc.sbuf_tensor("selC_s", [32, 16, 128], BF16))
    routeb = E(nc.sbuf_tensor("routeb", [128, NBLK, 8], BF16))
    seler = E(nc.sbuf_tensor("seler_s", [8, 128], F32))
    ones1 = E(nc.sbuf_tensor("ones1_s", [1, 128], F32))
    onesk = E(nc.sbuf_tensor("onesk_s", [128, 1], F32))
    selm = E(nc.sbuf_tensor("selm_s", [128, 8], F32))
    invm = E(nc.sbuf_tensor("invm_s", [128, 8], F32))
    negA = E(nc.sbuf_tensor("negA", [128, NBLK, NST], F32))
    oneD = E(nc.sbuf_tensor("oneD", [128, NBLK], F32))
    dvec_s = E(nc.sbuf_tensor("dvec_s", [128, NBLK], F32))
    bvec = E(nc.sbuf_tensor("bvec_s", [128, NBLK], F32))
    Hin = E(nc.sbuf_tensor("Hin", [128, NBLK, NST], F32))
    dt_rot = [E(nc.sbuf_tensor(f"dt{i}", [128, TC], F32)) for i in range(2)]
    dtx_rot = [E(nc.sbuf_tensor(f"dtx{i}", [128, TC], F32)) for i in range(2)]
    sp1_rot = [E(nc.sbuf_tensor(f"sp{i}", [128, TC], F32)) for i in range(2)]
    xf_rot = [E(nc.sbuf_tensor(f"xf{i}", [128, TC], F32)) for i in range(2)]
    sq_rot = [E(nc.sbuf_tensor("sq0", [128, TC], F32))] * 2
    lg_s = E(nc.sbuf_tensor("lg_s", [8, TC], F32))
    tmp8 = E(nc.sbuf_tensor("tmp8", [8, TC], F32))
    hf_s = E(nc.sbuf_tensor("hf_s", [128, TC], F32))
    rs_acc = E(nc.sbuf_tensor("rs_acc", [128, 4], F32))
    inj = E(nc.sbuf_tensor("inj", [128, 32], F32))
    xsb_rot = [E(nc.sbuf_tensor("xsb0", [128, TC], BF16))] * 2
    x1b_rot = [E(nc.sbuf_tensor(f"x1b{i}", [128, TC], BF16)) for i in range(2)]
    whb = E(nc.sbuf_tensor("whb", [128, TC], BF16))

    esW1 = contextlib.ExitStack()
    EW1 = esW1.enter_context
    wxpb = EW1(nc.sbuf_tensor("wxpb", [128, NBLK, 160], BF16, side="right"))
    xs_rot = [EW1(nc.sbuf_tensor(f"xs{i}", [128, TC], F32, side="right"))
              for i in range(2)]
    hcarryA = EW1(nc.sbuf_tensor("hcarryA", [128, NBLK, NST], F32, side="right"))
    rcarry = EW1(nc.sbuf_tensor("rcarry", [128, NBLK], F32, side="right"))
    Pc = xs_rot[0][:].rearrange("p (a n) -> p a n", n=NST)

    pb = [E(nc.psum_tensor(f"pb{i}", [128, TC], F32)) for i in range(8)]

    st_m = lg_s[0:1, :]
    st_v = tmp8[0:1, :]
    st_sc = hf_s[0:1, :]

    ENG = ("sync", "tensor", "scalar", "vector", "gpsimd")
    streams = {e: [] for e in ENG}
    cnt = {}
    waited = {e: {} for e in ENG}
    selfv = {e: {} for e in ENG}
    EV0 = None

    def op(eng, emit, waits=(), hw=(), inc=None, amt=1):
        ws = []
        for evt in hw:
            if evt is None:
                continue
            q, v = evt
            if v <= 0 or waited[eng].get(q, 0) >= v:
                continue
            waited[eng][q] = v
            ws.append((q, v))
        for evt in waits:
            if evt is None:
                continue
            q, v = evt
            if v <= 0:
                continue
            if q in DMAQ:
                v = cnt.get(q, 0)
            elif selfv[eng].get(q, 0) >= v:
                continue
            if waited[eng].get(q, 0) >= v:
                continue
            waited[eng][q] = v
            ws.append((q, v))
        ev = None
        incs = []
        if inc is not None:
            cnt[inc] = cnt.get(inc, 0) + amt
            if inc not in DMAQ:
                selfv[eng][inc] = cnt[inc]
            incs.append((inc, amt))
            ev = (inc, cnt[inc])
        streams[eng].append((ws, emit, incs))
        return ev

    def dma(queue_eng, out_ap, in_ap, waits=(), inc="ld"):
        return op(queue_eng,
                  lambda e, o=out_ap, i=in_ap: e.dma_start(out=o, in_=i),
                  waits=waits, inc=inc, amt=16)

    def emax(evs):
        best = {}
        for evt in evs:
            if evt is None:
                continue
            q, v = evt
            if best.get(q, -1) < v:
                best[q] = v
        return [(q, v) for q, v in best.items()]

    tr = {
        "xs_free": [EV0, EV0],
        "xsb_free": [EV0, EV0],
        "x1b_free": [EV0, EV0],
        "xf_free": [EV0, EV0],
        "dt_free": [EV0, EV0],
        "sp_free": [EV0, EV0],
        "pb01_free": [EV0, EV0],
        "pb23_free": EV0,
        "dA_free": [EV0] * NDA,
        "u_free": [EV0] * NUR,
        "dtx_ev": [EV0, EV0],
        "dtx_free": [[], []],
        "carry": [[EV0] * NBLK, [EV0] * NBLK],
        "bbc_users": [],
        "zS_free": [EV0] * NBLK,
        "z_ev": [EV0] * NBLK,
        "sq_free": [EV0, EV0],
        "pb67_free": EV0,
        "pb45_free": EV0,
        "dtin_ev": [EV0] * NCH,
        "stats_pe": EV0,
        "dbg_gate": EV0,
        "hin_ev": EV0,
        "inj_free": EV0,
        "injdve_free": EV0,
    }

    stg_wxp = zS[:, 17:27, :].rearrange("p a t -> p (a t)").rearrange(
        "p (blk m) -> p blk m", m=160)
    stg_misc = zS[:, 31, :]
    stg_affn = zS[:, 0:8, :].rearrange("p a t -> p (a t)").rearrange(
        "p (blk m) -> p blk m", m=128)
    stg_bwt = zS[:, 8:16, :].rearrange("p a t -> p (a t)")
    stg_route = zS[:, 16, 0:256].rearrange("p (blk m) -> p blk m", m=8)

    ld_wxp = dma("gpsimd", stg_wxp,
                 wxp_h.rearrange("(blk p) m -> p blk m", p=128), inc="gd")
    ld_alog = dma("gpsimd", stg_misc, alog_h, inc="gd")
    ld_affn = dma("gpsimd", stg_affn,
                  affn_h.rearrange("(blk p) m -> p blk m", p=128), inc="gd")
    ld_bwt = dma("gpsimd", stg_bwt, bwt_h, inc="gd")
    ld_route = dma("gpsimd", stg_route,
                   route_h.rearrange("(blk p) m -> p blk m", p=128), inc="gd")
    ld_dv = dma("gpsimd", dvec_s[:], dvec_h, inc="gd")
    ld_bv = dma("gpsimd", bvec[:], bvec_h, inc="gd")
    ld_sb = dma("gpsimd", selB[:],
                selbc_h[:, 0:2048].rearrange("k (j m) -> k j m", m=128),
                inc="gd")
    ld_sc = dma("gpsimd", selC[:],
                selbc_h[:, 2048:4096].rearrange("k (j m) -> k j m", m=128),
                inc="gd")
    ld_se = dma("gpsimd", seler[:], seler_h, inc="gd")
    ld_o1 = dma("gpsimd", ones1[:], ones1_h, inc="gd")
    ld_m1 = dma("gpsimd", selm[:], selm_h, inc="gd")
    ld_m2 = dma("gpsimd", invm[:], invm_h, inc="gd")

    cast_wxp = op("scalar", lambda e: e.copy(wxpb[:], stg_wxp),
                  waits=[ld_wxp], inc="act")
    ev_negA = op("scalar", lambda e: e.activation(
        negA[:].rearrange("p a n -> p (a n)"), stg_misc, AF.Exp),
        waits=[ld_alog], inc="act")
    ev_negA = op("vector", lambda e: e.tensor_scalar(
        negA[:].rearrange("p a n -> p (a n)"),
        negA[:].rearrange("p a n -> p (a n)"), -1.0, None, ALU.mult),
        waits=[ev_negA], inc="dve")
    ev_oneD = op("scalar", lambda e: e.activation(
        oneD[:], dvec_s[:], AF.Copy, bias=1.0), waits=[ld_dv], inc="act")
    stg_wdt = zS[:, 17:25, :].rearrange("p a t -> p (a t)")
    ld_wdt = dma("gpsimd", stg_wdt, wdt_h, waits=[cast_wxp], inc="gd")
    cast_wdt = op("scalar", lambda e: e.copy(
        wdtb[:].rearrange("p a m -> p (a m)"), stg_wdt),
        waits=[ld_wdt], inc="act")
    ev_init = op("vector", lambda e: e.memset(hcarryA[:], 0.0), inc="dve")
    for _i in range(NDA):
        ev_init0 = op("vector", lambda e, _i=_i: e.memset(
            dA_q[_i][:, :, 0:1].rearrange("p a o -> p (a o)"), 0.0),
            inc="dve")
    ev_init2 = op("vector", lambda e: e.memset(rcarry[:], 0.0), inc="dve")
    ev_init3 = op("vector", lambda e: e.memset(onesk[:], 1.0), inc="dve")
    for b in range(NBLK):
        tr["carry"][0][b] = ev_init

    HLD = {"dtgp": EV0, "bbc": EV0, "cbc": EV0, "inj_free": EV0}

    def emit_sweep_block(c, b):
        r2 = b % 2
        lx = dma("sync", xs_rot[r2][:],
                 x_h[b * 128:(b + 1) * 128, c * TC:(c + 1) * TC],
                 waits=[tr["xs_free"][r2]], inc="ld")
        tr["xs_free"][r2] = op(
            "scalar", lambda e, r2=r2: e.copy(xsb_rot[r2][:], xs_rot[r2][:]),
            waits=[lx, tr["xsb_free"][r2]], inc="act")
        w = [tr["xs_free"][r2], cast_wxp]
        if b == 0:
            w.append(tr["pb23_free"])

        def mm(e, b=b):
            e.matmul(pb[2][:], wxpb[:, b, 0:128], xsb_rot[b % 2][:],
                     start=(b == 0), stop=(b == NBLK - 1))
            return e.matmul(pb[3][0:32, :], wxpb[:, b, 128:160],
                            xsb_rot[b % 2][:],
                            start=(b == 0), stop=(b == NBLK - 1))
        ev = op("tensor", mm, waits=w, inc="pe")
        tr["xsb_free"] = [ev, ev]
        if b == NBLK - 1:
            op("scalar", lambda e, c=c: e.copy(dtin_all[:, c, :], pb[2][:]),
               waits=[ev], inc="act")
            ev2 = op("scalar", lambda e, c=c: e.copy(bc_all[:, c, :],
                                                     pb[3][0:32, :]),
                     inc="act")
            tr["dtin_ev"][c] = ev2
            tr["pb23_free"] = ev2

    def emit_bcast(c, with_C):
        users = emax(tr["bbc_users"])
        tr["bbc_users"] = []
        nj = 32 if with_C else 16
        last = EV0
        for j in range(nj):
            bank = 2 + j % 2
            w = [tr["dtin_ev"][c]]
            if j < 2:
                w.append(tr["pb23_free"])
            if last is not None:
                w.append(last)

            def mm(e, j=j, bank=bank, c=c):
                sel = selC[:, j - 16, :] if j >= 16 else selB[:, j, :]
                return e.matmul(pb[bank][:], sel, bc_all[:, c, :],
                                start=True, stop=True)
            pe = op("tensor", mm, waits=w + ([ld_sb, ld_sc] if j == 0 else []),
                    inc="pe")

            def ev(e, j=j, bank=bank):
                dst = C_bc[:, j - 16, :] if j >= 16 else B_bc[:, j, :]
                return e.copy(dst, pb[bank][:])
            last = op("scalar", ev, waits=[pe] + (users if j == 0 else []),
                      inc="act")
        tr["pb23_free"] = last
        return last

    FMM = {}

    def emit_front_mm(c, b):
        r2 = b % 2
        lx = dma("sync", xf_rot[r2][:],
                 x_h[b * 128:(b + 1) * 128, c * TC:(c + 1) * TC],
                 waits=[tr["xf_free"][r2]], inc="ld")

        def mm_dt(e, c=c, b=b, r2=r2):
            return e.matmul(pb[r2][:], wdtb[:, b, :], dtin_all[:, c, :],
                            start=True, stop=True)
        dtmm = op("tensor", mm_dt,
                  waits=[tr["dtin_ev"][c], tr["pb01_free"][r2], cast_wdt],
                  inc="pe")
        FMM[b % 2] = (lx, dtmm)

    def emit_front(c, b, phase):
        r2 = b % 2
        lx, dtmm = FMM[r2]

        def sp1f(e, b=b, r2=r2):
            e.activation(sp1_rot[r2][:], pb[r2][:], AF.Abs,
                         bias=bvec[:, b:b + 1])
            return e.activation(dt_rot[r2][:], pb[r2][:], AF.Relu,
                                bias=bvec[:, b:b + 1],
                                accum_out=rs_acc[:, r2:r2 + 1])
        spa = op("scalar", sp1f,
                 waits=[dtmm, tr["dt_free"][r2], tr["sp_free"][r2], ld_bv],
                 inc="act")
        spb = op("scalar", lambda e, r2=r2: e.activation(
            pb[r2][:], sp1_rot[r2][:], AF.Exp, scale=-1.0),
            hw=[spa], inc="act")
        spv = op("scalar", lambda e, r2=r2: e.activation(
            sp1_rot[r2][:], pb[r2][:], AF.Ln, bias=1.0,
            accum_out=rs_acc[:, 2 + r2:3 + r2]),
            hw=[spb], inc="act")
        tr["pb01_free"][r2] = spv

        def gadd(e, b=b, r2=r2, phase=phase):
            i = e.tensor_tensor(dt_rot[r2][:], dt_rot[r2][:], sp1_rot[r2][:],
                                ALU.add)
            if phase == "A":
                e.tensor_tensor(rs_acc[:, r2:r2 + 1], rs_acc[:, r2:r2 + 1],
                                rs_acc[:, 2 + r2:3 + r2], ALU.add)
                i = e.tensor_tensor(rcarry[:, b:b + 1], rcarry[:, b:b + 1],
                                    rs_acc[:, r2:r2 + 1], ALU.add)
            return i
        gv = op("gpsimd", gadd, waits=[spv, ev_init2], inc="gp")
        tr["sp_free"][r2] = gv
        dtxv = op("gpsimd",
                  lambda e, r2=r2: e.tensor_tensor(
                      dtx_rot[r2][:], dt_rot[r2][:], xf_rot[r2][:], ALU.mult),
                  waits=[lx] + tr["dtx_free"][r2], inc="gp")
        tr["dt_free"][r2] = dtxv
        tr["dtx_ev"][r2] = dtxv
        if phase == "A":
            tr["xf_free"][r2] = dtxv
        HLD["dtgp"] = gv
        return gv

    def emit_scanwork(c, b, phase, gctr):
        r2 = b % 2
        carry_t = hcarryA if phase == "A" else Hin
        pidx = 0 if phase == "A" else 1
        carry_ev = tr["carry"][pidx][b]

        dA0v = op("scalar", lambda e, b=b, r2=r2: e.activation(
            inj[:, 0:16], negA[:, b, :], AF.Exp,
            scale=dt_rot[r2][:, 0:1]),
            waits=[HLD["dtgp"], ev_negA, tr["inj_free"]], inc="act")
        injA = op("gpsimd", lambda e, b=b, carry_t=carry_t: e.tensor_tensor(
            inj[:, 16:32], inj[:, 0:16], carry_t[:, b, :], ALU.mult),
            waits=[dA0v, carry_ev, tr["injdve_free"]], inc="gp")

        last_gp_u = EV0
        last_dve_u = EV0
        sev = EV0
        for q in range(QN):
            g = gctr[0]
            gctr[0] += 1
            s = g % NDA
            su = g % NUR

            def dA_emit(e, b=b, q=q, s=s, r2=r2):
                e.activation(dA_q[s][:, 0, 1:TC], dt_rot[r2][:, 1:TC],
                             AF.Exp, scale=negA[:, b, 2 * q:2 * q + 1])
                return e.activation(dA_q[s][:, 1, 1:TC], dt_rot[r2][:, 1:TC],
                                    AF.Exp,
                                    scale=negA[:, b, 2 * q + 1:2 * q + 2])
            dAev = op("scalar", dA_emit,
                      waits=[HLD["dtgp"], tr["dA_free"][s], ev_negA, ev_init0],
                      inc="act")

            on_dve = (phase == "A" and q >= QN - DVEQ_A)
            eng = "vector" if on_dve else "gpsimd"
            qn = "dve" if on_dve else "gp"

            def u_emit(e, q=q, su=su, r2=r2):
                e.tensor_tensor(u_q[su][:, 0, :], dtx_rot[r2][:],
                                B_bc[:, 2 * q, :], ALU.mult)
                return e.tensor_tensor(u_q[su][:, 1, :], dtx_rot[r2][:],
                                       B_bc[:, 2 * q + 1, :], ALU.mult)
            uev = op(eng, u_emit,
                     waits=[tr["dtx_ev"][r2], tr["u_free"][su], HLD["bbc"]],
                     inc=qn)

            def fix_emit(e, q=q, su=su):
                u0 = u_q[su][:, :, 0:1].rearrange("p a o -> p (a o)")
                return e.tensor_tensor(u0, u0, inj[:, 16 + 2 * q:18 + 2 * q],
                                       ALU.add)
            fev = op(eng, fix_emit, waits=[injA],
                     hw=[uev] if on_dve else (), inc=qn)
            if on_dve:
                last_dve_u = fev
            else:
                last_gp_u = fev

            def scan_emit(e, q=q, s=s, su=su):
                return e.tensor_tensor_scan(
                    p_bf[:, 2 * q:2 * q + 2, :].rearrange("p a t -> p (a t)"),
                    dA_q[s][:].rearrange("p a t -> p (a t)"),
                    u_q[su][:].rearrange("p a t -> p (a t)"),
                    0.0, ALU.mult, ALU.add)
            sev = op("vector", scan_emit,
                     waits=[uev, fev, dAev], inc="dve")
            tr["dA_free"][s] = sev
            tr["u_free"][su] = sev
        tr["inj_free"] = sev
        if last_dve_u is not None:
            tr["injdve_free"] = last_dve_u
        tr["dtx_free"][r2] = [ev for ev in (last_gp_u, last_dve_u)
                              if ev is not None]
        tr["bbc_users"].append(sev)

        cev = op("vector", lambda e, b=b, carry_t=carry_t: e.tensor_copy(
            carry_t[:, b, :],
            p_bf[:, :, TC - 1:TC].rearrange("p a o -> p (a o)")),
            waits=[tr["dbg_gate"]], hw=[sev], inc="dve")
        tr["carry"][pidx][b] = cev

        if phase == "B":
            def cm(e):
                e.tensor_tensor(p_bf[:, 0:8, :].rearrange("p a t -> p (a t)"),
                                p_bf[:, 0:8, :].rearrange("p a t -> p (a t)"),
                                C_bc[:, 0:8, :].rearrange("p a t -> p (a t)"),
                                ALU.mult)
                return e.tensor_tensor(
                    p_bf[:, 8:16, :].rearrange("p a t -> p (a t)"),
                    p_bf[:, 8:16, :].rearrange("p a t -> p (a t)"),
                    C_bc[:, 8:16, :].rearrange("p a t -> p (a t)"), ALU.mult)
            cmev = op("vector", cm, waits=[HLD["cbc"]], hw=[cev], inc="dve")
            tr["bbc_users"].append(cmev)
            t1 = op("vector", lambda e: e.tensor_tensor(
                p_bf[:, 0:8, :].rearrange("p a t -> p (a t)"),
                p_bf[:, 0:8, :].rearrange("p a t -> p (a t)"),
                p_bf[:, 8:16, :].rearrange("p a t -> p (a t)"), ALU.add),
                hw=[cmev], inc="dve")
            t2 = op("vector", lambda e: e.tensor_tensor(
                p_bf[:, 0:4, :].rearrange("p a t -> p (a t)"),
                p_bf[:, 0:4, :].rearrange("p a t -> p (a t)"),
                p_bf[:, 4:8, :].rearrange("p a t -> p (a t)"), ALU.add),
                hw=[t1], inc="dve")
            t3 = op("vector", lambda e: e.tensor_tensor(
                p_bf[:, 0:2, :].rearrange("p a t -> p (a t)"),
                p_bf[:, 0:2, :].rearrange("p a t -> p (a t)"),
                p_bf[:, 2:4, :].rearrange("p a t -> p (a t)"), ALU.add),
                hw=[t2], inc="dve")
            t4 = op("vector", lambda e, r2=r2: e.tensor_tensor(
                sq_rot[r2][:], p_bf[:, 0, :], p_bf[:, 1, :], ALU.add),
                waits=[tr["sq_free"][r2]], hw=[t3], inc="dve")
            zev = op("vector", lambda e, b=b, r2=r2: e.scalar_tensor_tensor(
                zS[:, b, :], xf_rot[r2][:], oneD[:, b:b + 1], sq_rot[r2][:],
                ALU.mult, ALU.add),
                waits=[ev_oneD, tr["zS_free"][b]], hw=[t4], inc="dve")
            tr["z_ev"][b] = zev
            tr["xf_free"][r2] = zev
            sqev = op("scalar", lambda e, b=b, r2=r2: e.activation(
                sq_rot[r2][:], zS[:, b, :], AF.Square),
                waits=[zev, tr["sq_free"][r2]], inc="act")

            def smm(e, b=b, r2=r2):
                e.matmul(pb[6][0:1, :], onesk[:], zS[:, b, :],
                         start=(b == 0), stop=(b == NBLK - 1))
                return e.matmul(pb[7][0:1, :], onesk[:], sq_rot[r2][:],
                                start=(b == 0), stop=(b == NBLK - 1))
            w = [sqev, ev_init3]
            if b == 0:
                w.append(tr["pb67_free"])
            pev = op("tensor", smm, waits=w, inc="pe")
            tr["sq_free"] = [pev, pev]
            if b == NBLK - 1:
                tr["stats_pe"] = pev

    def emit_stats_math():
        m1 = op("vector", lambda e: e.tensor_scalar(
            st_m, pb[6][0:1, :], 1.0 / DM, None, ALU.mult),
            waits=[tr["stats_pe"]], inc="dve")
        m2 = op("vector", lambda e: e.tensor_scalar(
            st_v, pb[7][0:1, :], 1.0 / DM, None, ALU.mult),
            hw=[m1], inc="dve")
        m3 = op("vector", lambda e: e.tensor_tensor(
            st_sc, st_m, st_m, ALU.mult), hw=[m2], inc="dve")
        m4 = op("vector", lambda e: e.tensor_tensor(
            st_v, st_v, st_sc, ALU.subtract), hw=[m3], inc="dve")
        m5 = op("vector", lambda e: e.tensor_scalar(
            st_v, st_v, EPS, None, ALU.add), hw=[m4], inc="dve")
        r1 = op("scalar", lambda e: e.activation(st_v, st_v, AF.Ln),
                waits=[m5], inc="act")
        r2 = op("scalar", lambda e: e.activation(st_v, st_v, AF.Exp,
                                                 scale=-0.5),
                hw=[r1], inc="act")
        mr = op("vector", lambda e: e.tensor_tensor(
            st_m, st_m, st_v, ALU.mult), waits=[r2], inc="dve")

        def bc(e):
            e.matmul(pb[4][:], ones1[:], st_v, start=True, stop=True)
            return e.matmul(pb[5][:], ones1[:], st_m, start=True, stop=True)
        bcp = op("tensor", bc, waits=[mr, tr["pb45_free"], ld_o1], inc="pe")
        return mr, bcp

    for b in range(NBLK):
        emit_sweep_block(0, b)
    if dbg:
        dma("gpsimd", dbgODm_h[0], oneD[:], waits=[tr["dtin_ev"][0]],
            inc="gd")
    for c in range(NCH):
        HLD["bbc"] = emit_bcast(c, with_C=False)
        gctr = [0]
        for b in range(NBLK):
            emit_front_mm(c, b)
            emit_front(c, b, "A")
            emit_scanwork(c, b, "A", gctr)
            if c + 1 < NCH:
                emit_sweep_block(c + 1, b)
        if dbg:
            if c == 0:
                dma("gpsimd", dbgODm_h[1], oneD[:],
                    waits=emax(tr["carry"][0]), inc="gd")
            if c == NCH - 1:
                dma("gpsimd", dbgODm_h[2], oneD[:],
                    waits=emax(tr["carry"][0]), inc="gd")
            dgev = dma("gpsimd", dbgSc_h[c],
                       hcarryA[:].rearrange("p a b -> p (a b)"),
                       waits=emax(tr["carry"][0]), inc="gd")
            tr["dbg_gate"] = dgev
            if c == 0:
                dma("gpsimd", dbgDT_h, dt_rot[1][:], waits=[dgev], inc="gd")
                dma("gpsimd", dbgDTX_h, dtx_rot[1][:], inc="gd")
                dma("gpsimd", dbgB_h,
                    B_bc[:].rearrange("p a t -> p (a t)"), inc="gd")
                dgev = dma("gpsimd", dbgP_h,
                           p_bf[:].rearrange("p a t -> p (a t)"), inc="gd")
                tr["dbg_gate"] = dgev

    pc_ev = EV0
    for b in range(NBLK):
        pc_ev = op("scalar", lambda e, b=b: e.activation(
            Pc[:, b, :], negA[:, b, :], AF.Exp, scale=rcarry[:, b:b + 1]),
            waits=([tr["sp_free"][0], tr["sp_free"][1]] if b == 0 else []),
            inc="act")
    cc1 = dma("gpsimd", cc_in[0:DM, :].rearrange("(blk p) n -> p blk n", p=128),
              hcarryA[:],
              waits=emax(tr["carry"][0] + [tr["dbg_gate"]]), inc="cc")
    cc2 = dma("gpsimd",
              cc_in[DM:2 * DM, :].rearrange("(blk p) n -> p blk n", p=128),
              Pc, waits=[pc_ev], inc="cc")
    ccv = op("gpsimd", lambda e: e.collective_compute(
        "AllGather", ALU.bypass, replica_groups=[list(range(NCORES))],
        ins=[cc_in.ap().opt()], outs=[cc_out.ap().opt()]),
        waits=[cc1, cc2], inc="cc")
    ccf = dma("gpsimd", ccf_in.ap(), cc_in[0:1, :], waits=[ccv], inc="cc")
    ccv = op("gpsimd", lambda e: e.collective_compute(
        "AllReduce", ALU.add, replica_groups=[list(range(NCORES))],
        ins=[ccf_in.ap().opt()], outs=[ccf_out.ap().opt()]),
        waits=[ccf], inc="cc")
    if dbg:
        dma("gpsimd", dbgS_h, hcarryA[:].rearrange("p a b -> p (a b)"),
            waits=[ccv], inc="gd")

    esW1.close()
    esW2 = contextlib.ExitStack()
    EW2 = esW2.enter_context
    gS1 = EW2(nc.sbuf_tensor("gS1", [128, NBLK * NST], F32, side="right"))
    gP1 = EW2(nc.sbuf_tensor("gP1", [128, NBLK * NST], F32, side="right"))
    pprime = EW2(nc.sbuf_tensor("pprime", [128, NBLK * NST], F32, side="right"))

    hv = op("vector", lambda e: e.memset(Hin[:], 0.0), waits=[ccv],
            inc="dve")
    Hf = Hin[:].rearrange("p a b -> p (a b)")
    for j in range(NCORES):
        base = j * 2 * DM
        gjs = dma("gpsimd",
                  gS1[:].rearrange("p (blk n) -> p blk n", n=NST),
                  cc_out[base:base + DM, :].rearrange(
                      "(blk p) n -> p blk n", p=128),
                  waits=[ccv, hv], inc="gd")
        gjp = dma("gpsimd",
                  gP1[:].rearrange("p (blk n) -> p blk n", n=NST),
                  cc_out[base + DM:base + 2 * DM, :].rearrange(
                      "(blk p) n -> p blk n", p=128), inc="gd")
        hv = op("vector", lambda e, j=j: e.tensor_scalar(
            pprime[:], gP1[:], selm[:, j:j + 1], invm[:, j:j + 1],
            ALU.mult, ALU.add), waits=[ld_m1, ld_m2, gjs, gjp], hw=[hv],
            inc="dve")
        hv = op("vector", lambda e: e.tensor_tensor(Hf, Hf, pprime[:],
                                                    ALU.mult),
                hw=[hv], inc="dve")
        hv = op("vector", lambda e, j=j: e.tensor_scalar(
            pprime[:], gS1[:], selm[:, j:j + 1], None, ALU.mult),
            hw=[hv], inc="dve")
        hv = op("vector", lambda e: e.tensor_tensor(Hf, Hf, pprime[:],
                                                    ALU.add),
                hw=[hv], inc="dve")
    hv = op("vector", lambda e: e.nop(), hw=[hv], inc="dve")
    tr["hin_ev"] = hv
    if dbg:
        dma("gpsimd", dbgH_h, Hin[:].rearrange("p a b -> p (a b)"),
            waits=[hv], inc="gd")
        dma("gpsimd", dbgODm_h[3], oneD[:], waits=[hv], inc="gd")

    esW2.close()
    esW3 = contextlib.ExitStack()
    EW3 = esW3.enter_context
    affnb = EW3(nc.sbuf_tensor("affnb", [128, NBLK, 128], BF16, side="right"))
    bwtb = EW3(nc.sbuf_tensor("bwtb", [128, NBLK, 128], BF16, side="right"))
    wc1 = op("scalar", lambda e: e.copy(affnb[:], stg_affn),
             waits=[ld_affn, hv], inc="act")
    wc2 = op("scalar", lambda e: e.copy(
        bwtb[:].rearrange("p a m -> p (a m)"), stg_bwt),
        waits=[ld_bwt, hv], inc="act")
    wc3 = op("scalar", lambda e: e.copy(routeb[:], stg_route),
             waits=[ld_route], inc="act")
    for b in range(NBLK):
        tr["zS_free"][b] = wc3
        tr["carry"][1][b] = hv

    for _e in ("tensor", "scalar", "vector", "gpsimd"):
        op(_e, lambda e: e.nop(), waits=[hv, ccv])

    for c in range(NCH):
        cb = emit_bcast(c, with_C=True)
        HLD["bbc"] = cb
        HLD["cbc"] = cb
        gctr = [0]
        emit_front_mm(c, 0)
        FMM2 = dict(FMM)
        for b in range(NBLK):
            FMM[b % 2] = FMM2[b % 2]
            if b + 1 < NBLK:
                emit_front_mm(c, b + 1)
                FMM2[(b + 1) % 2] = FMM[(b + 1) % 2]
                FMM[b % 2] = FMM2[b % 2]
            emit_front(c, b, "B")
            emit_scanwork(c, b, "B", gctr)

        dbgz_ev = EV0
        if dbg and c == 0:
            dbgz_ev = dma("gpsimd", dbgZ_h,
                          zS[:].rearrange("p a t -> p (a t)"),
                          waits=emax(tr["z_ev"]), inc="gd")
        mr1, bc1 = emit_stats_math()
        ln1 = dbgz_ev
        ffn_pe = EV0
        for b in range(NBLK):
            ln1 = op("vector", lambda e, b=b: e.tensor_tensor(
                zS[:, b, :], zS[:, b, :], pb[4][:], ALU.mult),
                waits=[bc1, tr["z_ev"][b]], hw=([ln1] if ln1 else ()),
                inc="dve")
            ln1 = op("vector", lambda e, b=b: e.tensor_tensor(
                zS[:, b, :], zS[:, b, :], pb[5][:], ALU.subtract),
                hw=[ln1], inc="dve")
            xc = op("scalar", lambda e, b=b: e.copy(x1b_rot[b % 2][:],
                                                    zS[:, b, :]),
                    waits=[ln1, tr["x1b_free"][b % 2]], inc="act")

            def fmm(e, b=b):
                e.matmul(pb[6][0:8, :], routeb[:, b, :], x1b_rot[b % 2][:],
                         start=(b == 0), stop=(b == NBLK - 1))
                return e.matmul(pb[7][:], affnb[:, b, :], x1b_rot[b % 2][:],
                                start=(b == 0), stop=(b == NBLK - 1))
            ffn_pe = op("tensor", fmm, waits=[xc, wc1, wc2, wc3], inc="pe")
            tr["x1b_free"][b % 2] = ffn_pe
        tr["pb45_free"] = ln1

        dbgx_ev = EV0
        if dbg and c == 0:
            dbgx_ev = dma("gpsimd", dbgX1_h,
                          zS[:].rearrange("p a t -> p (a t)"),
                          waits=[ln1], inc="gd")
        lge = op("scalar", lambda e: e.copy(lg_s[:], pb[6][0:8, :]),
                 waits=[ffn_pe], inc="act")
        if dbg and c == 0:
            lge = dma("gpsimd", dbgLG_h, lg_s[:], waits=[lge], inc="gd")
        s1 = dma("sync", tmp8[0:4, :], lg_s[4:8, :], waits=[lge], inc="sd")
        v1 = op("vector", lambda e: e.tensor_tensor(
            tmp8[0:4, :], lg_s[0:4, :], tmp8[0:4, :], ALU.max),
            waits=[s1], inc="dve")
        s2 = dma("sync", hf_s[0:2, :], tmp8[2:4, :], waits=[v1], inc="sd")
        v2 = op("vector", lambda e: e.tensor_tensor(
            tmp8[0:2, :], tmp8[0:2, :], hf_s[0:2, :], ALU.max),
            waits=[s2], inc="dve")
        s3 = dma("sync", hf_s[0:1, :], tmp8[1:2, :], waits=[v2], inc="sd")
        v3 = op("vector", lambda e: e.tensor_tensor(
            tmp8[0:1, :], tmp8[0:1, :], hf_s[0:1, :], ALU.max),
            waits=[s3], inc="dve")
        m8 = op("tensor", lambda e: e.matmul(
            pb[4][0:8, :], ones1[:, 0:8], tmp8[0:1, :], start=True, stop=True),
            waits=[v3], inc="pe")
        v4 = op("vector", lambda e: e.tensor_tensor(
            tmp8[:], lg_s[:], pb[4][0:8, :], ALU.is_equal),
            waits=[m8], inc="dve")
        v5 = op("vector", lambda e: e.tensor_tensor(
            tmp8[:], tmp8[:], lg_s[:], ALU.mult), hw=[v4], inc="dve")
        if dbg and c == 0:
            v5 = dma("gpsimd", dbgRW_h, tmp8[:], waits=[v5], inc="gd")
        mkp = op("tensor", lambda e: e.matmul(
            pb[5][:], seler[:], tmp8[:], start=True, stop=True),
            waits=[v5, ld_se], inc="pe")
        hfe = op("scalar", lambda e: e.copy(hf_s[:], pb[7][:]),
                 waits=[mkp], inc="act")
        whv = op("vector", lambda e: e.tensor_tensor(
            hf_s[:], hf_s[:], pb[5][:], ALU.mult), waits=[hfe], inc="dve")
        if dbg and c == 0:
            whv = dma("gpsimd", dbgWH_h, hf_s[:], waits=[whv], inc="gd")
        whc = op("scalar", lambda e: e.copy(whb[:], hf_s[:]),
                 waits=[whv], inc="act")
        if dbg and c == 0:
            z2p = [whv, whv]

        z2p = [whv, whv]
        sq2e = [EV0, EV0]
        for b in range(NBLK):
            bank = 4 + b % 2

            def fo(e, b=b, bank=bank):
                return e.matmul(pb[bank][:], bwtb[:, b, :], whb[:],
                                start=True, stop=True)
            fop = op("tensor", fo, waits=[whc, z2p[b % 2]], inc="pe")
            z2 = op("vector", lambda e, b=b, bank=bank: e.tensor_tensor(
                zS[:, b, :], zS[:, b, :], pb[bank][:], ALU.add),
                waits=[fop], inc="dve")
            z2p[b % 2] = z2
            sq2 = op("scalar", lambda e, b=b: e.activation(
                sq_rot[b % 2][:], zS[:, b, :], AF.Square),
                waits=[z2, sq2e[b % 2]], inc="act")

            def smm2(e, b=b):
                e.matmul(pb[6][0:1, :], onesk[:], zS[:, b, :],
                         start=(b == 0), stop=(b == NBLK - 1))
                return e.matmul(pb[7][0:1, :], onesk[:], sq_rot[b % 2][:],
                                start=(b == 0), stop=(b == NBLK - 1))
            st2 = op("tensor", smm2, waits=[sq2], inc="pe")
            sq2e = [st2, st2]
            if b == NBLK - 1:
                tr["stats_pe"] = st2
        tr["sq_free"] = [tr["stats_pe"], tr["stats_pe"]]

        dbgz2_ev = EV0
        if dbg and c == 0:
            dbgz2_ev = dma("gpsimd", dbgZ2_h,
                           zS[:].rearrange("p a t -> p (a t)"),
                           waits=emax(z2p), inc="gd")
        mr2, bc2 = emit_stats_math()
        tr["pb67_free"] = mr2
        ln2 = dbgz2_ev
        for b in range(NBLK):
            ln2 = op("vector", lambda e, b=b: e.tensor_tensor(
                zS[:, b, :], zS[:, b, :], pb[4][:], ALU.mult),
                waits=[bc2], hw=([ln2] if ln2 else ()), inc="dve")
            ln2 = op("vector", lambda e, b=b: e.tensor_tensor(
                zS[:, b, :], zS[:, b, :], pb[5][:], ALU.subtract),
                hw=[ln2], inc="dve")
            stv = dma("sync",
                      out_h[b * 128:(b + 1) * 128, c * TC:(c + 1) * TC],
                      zS[:, b, :], waits=[ln2], inc="st")
            tr["zS_free"][b] = stv
        tr["pb45_free"] = ln2
        bwaits = [(q, cnt.get(q, 0)) for q in ("pe", "act", "dve", "gp")]
        for _e in ("tensor", "scalar", "vector", "gpsimd"):
            op(_e, lambda e: e.nop(), waits=list(bwaits))
        if dbg and c == 0:
            dma("gpsimd", dbgODm_h[4], oneD[:], waits=[ln2], inc="gd")

    if dbg:
        dma("gpsimd", dbgOD_h, oneD[:],
            waits=[("st", cnt.get("st", 0))], inc="gd")
        dma("gpsimd", dbgDV_h, dvec_s[:], inc="gd")

    op("gpsimd", lambda e: e.nop(),
       waits=[("st", cnt.get("st", 0)), ("gd", cnt.get("gd", 0)),
              ("ld", cnt.get("ld", 0)), ("sd", cnt.get("sd", 0)),
              ("cc", cnt.get("cc", 0))])

    sems = {q: E(nc.semaphore("q_" + q)) for q in sorted(cnt)}
    blk = E(nc.Block())
    for eng in ENG:
        sec = getattr(blk, eng)

        def body(e, eng=eng):
            for (ws, emit, incs) in streams[eng]:
                for (q, v) in ws:
                    e.wait_ge(sems[q], v)
                inst = emit(e)
                for (q, amt) in incs:
                    inst.then_inc(sems[q], amt)
        sec(body)

    esW3.close()
    es.close()
    return nc


def _host_prep(L0_attn_x, L0_attn_A_log, L0_attn_D, L0_attn_x_proj_w,
               L0_attn_dt_proj_w, L0_attn_dt_proj_b, L0_ffn_route_w, L0_ffn_A,
               L0_ffn_Bw):
    import ml_dtypes
    x = np.asarray(L0_attn_x, dtype=np.float32)
    A_log = np.asarray(L0_attn_A_log, dtype=np.float32)
    Dv = np.asarray(L0_attn_D, dtype=np.float32)
    xp_w = np.asarray(L0_attn_x_proj_w, dtype=np.float32)
    dt_w = np.asarray(L0_attn_dt_proj_w, dtype=np.float32)
    dt_b = np.asarray(L0_attn_dt_proj_b, dtype=np.float32)
    route_w = np.asarray(L0_ffn_route_w, dtype=np.float32)
    A_ffn = np.asarray(L0_ffn_A, dtype=np.float32)
    Bw = np.asarray(L0_ffn_Bw, dtype=np.float32)

    wxp = np.ascontiguousarray(xp_w.T)
    wdt = np.ascontiguousarray(dt_w.T)
    alog = np.ascontiguousarray(
        A_log.reshape(NBLK, 128, NST).transpose(1, 0, 2)).reshape(
        128, NBLK * NST)
    dvec = np.ascontiguousarray(Dv.reshape(NBLK, 128).T)
    bvec = np.ascontiguousarray(dt_b.reshape(NBLK, 128).T)
    route = np.ascontiguousarray(route_w.T)
    affn = np.ascontiguousarray(np.transpose(A_ffn, (2, 0, 1)).reshape(DM, 128))
    bwt = np.ascontiguousarray(np.transpose(Bw, (0, 2, 1)).reshape(128, DM))
    selbc = np.zeros((32, 32 * 128), dtype=ml_dtypes.bfloat16)
    for j in range(32):
        selbc[j, j * 128:(j + 1) * 128] = 1.0
    seler = np.zeros((8, 128), dtype=np.float32)
    for e in range(8):
        seler[e, e * 16:(e + 1) * 16] = 1.0
    ones1 = np.ones((1, 128), dtype=np.float32)

    in_maps = []
    for cid in range(NCORES):
        bb, qq = cid // 4, cid % 4
        xc = np.ascontiguousarray(x[bb, qq * TLOC:(qq + 1) * TLOC, :].T)
        sel = np.zeros(8, dtype=np.float32)
        for j in range(NCORES):
            if j // 4 == bb and j % 4 < qq:
                sel[j] = 1.0
        in_maps.append({
            "x": xc, "wxp": wxp, "wdt": wdt, "alog": alog, "dvec": dvec,
            "bvec": bvec, "route": route, "affn": affn, "bwt": bwt,
            "selbc": selbc, "seler": seler, "ones1": ones1,
            "selm": np.ascontiguousarray(np.tile(sel, (128, 1))),
            "invm": np.ascontiguousarray(np.tile(1.0 - sel, (128, 1))),
        })
    return in_maps


def kernel(L0_attn_x, L0_attn_A_log, L0_attn_D, L0_attn_x_proj_w,
           L0_attn_dt_proj_w, L0_attn_dt_proj_b, L0_ffn_route_w, L0_ffn_A,
           L0_ffn_Bw):
    in_maps = _host_prep(L0_attn_x, L0_attn_A_log, L0_attn_D, L0_attn_x_proj_w,
                         L0_attn_dt_proj_w, L0_attn_dt_proj_b, L0_ffn_route_w,
                         L0_ffn_A, L0_ffn_Bw)
    nc = build_nc()
    res = run_bass_kernel_spmd(nc, in_maps, core_ids=list(range(NCORES)))
    out = np.empty((2, 8192, DM), dtype=np.float32)
    for cid in range(NCORES):
        bb, qq = cid // 4, cid % 4
        out[bb, qq * TLOC:(qq + 1) * TLOC, :] = res.results[cid]["o"].T
    return out


# version 11 (final)
# speedup vs baseline: 1.2891x; 1.0077x over previous
import contextlib
import numpy as np
import concourse.bass as bass
import concourse.mybir as mybir
from concourse.bass_utils import run_bass_kernel_spmd

F32 = mybir.dt.float32
BF16 = mybir.dt.bfloat16
AF = mybir.ActivationFunctionType
ALU = mybir.AluOpType

NCORES = 8
DM = 4096
NBLK = 32
TLOC = 2048
TC = 512
NCH = TLOC // TC
NST = 16
EPS = 1e-5
NQ = 2
QN = NST // NQ
NDA = 3
NUR = 2
DVEQ_A = 2

DMAQ = ("ld", "st", "sd", "gd", "cc")


def build_nc(dbg=False):
    nc = bass.Bass("TRN2", target_bir_lowering=False, debug=False)

    x_h = nc.dram_tensor("x", [DM, TLOC], F32, kind="ExternalInput").ap()
    wxp_h = nc.dram_tensor("wxp", [DM, 160], F32, kind="ExternalInput").ap()
    wdt_h = nc.dram_tensor("wdt", [128, DM], F32, kind="ExternalInput").ap()
    alog_h = nc.dram_tensor("alog", [128, NBLK * NST], F32, kind="ExternalInput").ap()
    dvec_h = nc.dram_tensor("dvec", [128, NBLK], F32, kind="ExternalInput").ap()
    bvec_h = nc.dram_tensor("bvec", [128, NBLK], F32, kind="ExternalInput").ap()
    route_h = nc.dram_tensor("route", [DM, 8], F32, kind="ExternalInput").ap()
    affn_h = nc.dram_tensor("affn", [DM, 128], F32, kind="ExternalInput").ap()
    bwt_h = nc.dram_tensor("bwt", [128, DM], F32, kind="ExternalInput").ap()
    selbc_h = nc.dram_tensor("selbc", [32, 32 * 128], BF16, kind="ExternalInput").ap()
    seler_h = nc.dram_tensor("seler", [8, 128], F32, kind="ExternalInput").ap()
    ones1_h = nc.dram_tensor("ones1", [1, 128], F32, kind="ExternalInput").ap()
    selm_h = nc.dram_tensor("selm", [128, 8], F32, kind="ExternalInput").ap()
    invm_h = nc.dram_tensor("invm", [128, 8], F32, kind="ExternalInput").ap()
    out_h = nc.dram_tensor("o", [DM, TLOC], F32, kind="ExternalOutput").ap()

    cc_in = nc.dram_tensor("cc_in", [2 * DM, NST], F32)
    cc_out = nc.dram_tensor("cc_out", [2 * DM * NCORES, NST], F32)
    ccf_in = nc.dram_tensor("ccf_in", [1, NST], F32)
    ccf_out = nc.dram_tensor("ccf_out", [1, NST], F32)
    if dbg:
        dbgS_h = nc.dram_tensor("dbg_S", [128, NBLK * NST], F32,
                                kind="ExternalOutput").ap()
        dbgSc_h = [nc.dram_tensor(f"dbg_Sc{i}", [128, NBLK * NST], F32,
                                  kind="ExternalOutput").ap() for i in range(NCH)]
        dbgH_h = nc.dram_tensor("dbg_H", [128, NBLK * NST], F32,
                                kind="ExternalOutput").ap()
        dbgDT_h = nc.dram_tensor("dbg_dt", [128, TC], F32,
                                 kind="ExternalOutput").ap()
        dbgDTX_h = nc.dram_tensor("dbg_dtx", [128, TC], BF16,
                                  kind="ExternalOutput").ap()
        dbgB_h = nc.dram_tensor("dbg_B", [128, NST * TC], BF16,
                                kind="ExternalOutput").ap()
        dbgP_h = nc.dram_tensor("dbg_p", [128, NST * TC], BF16,
                                kind="ExternalOutput").ap()
        dbgZ_h = nc.dram_tensor("dbg_z", [128, NBLK * TC], F32,
                                kind="ExternalOutput").ap()
        dbgX1_h = nc.dram_tensor("dbg_x1", [128, NBLK * TC], F32,
                                 kind="ExternalOutput").ap()
        dbgLG_h = nc.dram_tensor("dbg_lg", [8, TC], F32,
                                 kind="ExternalOutput").ap()
        dbgRW_h = nc.dram_tensor("dbg_rw", [8, TC], F32,
                                 kind="ExternalOutput").ap()
        dbgWH_h = nc.dram_tensor("dbg_wh", [128, TC], F32,
                                 kind="ExternalOutput").ap()
        dbgZ2_h = nc.dram_tensor("dbg_z2", [128, NBLK * TC], F32,
                                 kind="ExternalOutput").ap()
        dbgOD_h = nc.dram_tensor("dbg_od", [128, NBLK], F32,
                                 kind="ExternalOutput").ap()
        dbgODm_h = [nc.dram_tensor(f"dbg_od{i}", [128, NBLK], F32,
                                   kind="ExternalOutput").ap()
                    for i in range(5)]
        dbgDV_h = nc.dram_tensor("dbg_dv", [128, NBLK], F32,
                                 kind="ExternalOutput").ap()

    es = contextlib.ExitStack()
    E = es.enter_context

    zS = E(nc.sbuf_tensor("zS", [128, NBLK, TC], F32))
    p_bf = E(nc.sbuf_tensor("p_bf", [128, NST, TC], BF16))
    B_bc = E(nc.sbuf_tensor("B_bc", [128, NST, TC], BF16))
    C_bc = E(nc.sbuf_tensor("C_bc", [128, NST, TC], BF16))
    dA_q = [E(nc.sbuf_tensor(f"dA{i}", [128, NQ, TC], F32)) for i in range(NDA)]
    u_q = [E(nc.sbuf_tensor(f"u{i}", [128, NQ, TC], F32)) for i in range(NUR)]
    dtin_all = E(nc.sbuf_tensor("dtin_all", [128, NCH, TC], BF16))
    bc_all = E(nc.sbuf_tensor("bc_all", [32, NCH, TC], BF16))
    wdtb = E(nc.sbuf_tensor("wdtb", [128, NBLK, 128], BF16))
    selB = E(nc.sbuf_tensor("selB_s", [32, 16, 128], BF16))
    selC = E(nc.sbuf_tensor("selC_s", [32, 16, 128], BF16))
    routeb = E(nc.sbuf_tensor("routeb", [128, NBLK, 8], BF16))
    seler = E(nc.sbuf_tensor("seler_s", [8, 128], F32))
    ones1 = E(nc.sbuf_tensor("ones1_s", [1, 128], F32))
    onesk = E(nc.sbuf_tensor("onesk_s", [128, 1], F32))
    selm = E(nc.sbuf_tensor("selm_s", [128, 8], F32))
    invm = E(nc.sbuf_tensor("invm_s", [128, 8], F32))
    negA = E(nc.sbuf_tensor("negA", [128, NBLK, NST], F32))
    oneD = E(nc.sbuf_tensor("oneD", [128, NBLK], F32))
    dvec_s = E(nc.sbuf_tensor("dvec_s", [128, NBLK], F32))
    bvec = E(nc.sbuf_tensor("bvec_s", [128, NBLK], F32))
    Hin = E(nc.sbuf_tensor("Hin", [128, NBLK, NST], F32))
    dt_rot = [E(nc.sbuf_tensor(f"dt{i}", [128, TC], F32)) for i in range(2)]
    dtx_rot = [E(nc.sbuf_tensor(f"dtx{i}", [128, TC], F32)) for i in range(2)]
    sp1_rot = [E(nc.sbuf_tensor(f"sp{i}", [128, TC], F32)) for i in range(2)]
    xf_rot = [E(nc.sbuf_tensor(f"xf{i}", [128, TC], F32)) for i in range(2)]
    sq_rot = [E(nc.sbuf_tensor("sq0", [128, TC], F32))] * 2
    lg_s = E(nc.sbuf_tensor("lg_s", [8, TC], F32))
    tmp8 = E(nc.sbuf_tensor("tmp8", [8, TC], F32))
    hf_s = E(nc.sbuf_tensor("hf_s", [128, TC], F32))
    rs_acc = E(nc.sbuf_tensor("rs_acc", [128, 4], F32))
    inj = E(nc.sbuf_tensor("inj", [128, 32], F32))
    xsb_rot = [E(nc.sbuf_tensor("xsb0", [128, TC], BF16))] * 2
    x1b_rot = [E(nc.sbuf_tensor(f"x1b{i}", [128, TC], BF16)) for i in range(2)]
    whb = E(nc.sbuf_tensor("whb", [128, TC], BF16))

    esW1 = contextlib.ExitStack()
    EW1 = esW1.enter_context
    wxpb = EW1(nc.sbuf_tensor("wxpb", [128, NBLK, 160], BF16, side="right"))
    xs_rot = [EW1(nc.sbuf_tensor(f"xs{i}", [128, TC], F32, side="right"))
              for i in range(2)]
    hcarryA = EW1(nc.sbuf_tensor("hcarryA", [128, NBLK, NST], F32, side="right"))
    rcarry = EW1(nc.sbuf_tensor("rcarry", [128, NBLK], F32, side="right"))
    Pc = xs_rot[0][:].rearrange("p (a n) -> p a n", n=NST)

    pb = [E(nc.psum_tensor(f"pb{i}", [128, TC], F32)) for i in range(8)]

    st_m = lg_s[0:1, :]
    st_v = tmp8[0:1, :]
    st_sc = hf_s[0:1, :]

    ENG = ("sync", "tensor", "scalar", "vector", "gpsimd")
    streams = {e: [] for e in ENG}
    cnt = {}
    waited = {e: {} for e in ENG}
    selfv = {e: {} for e in ENG}
    EV0 = None

    def op(eng, emit, waits=(), hw=(), inc=None, amt=1):
        ws = []
        for evt in hw:
            if evt is None:
                continue
            q, v = evt
            if v <= 0 or waited[eng].get(q, 0) >= v:
                continue
            waited[eng][q] = v
            ws.append((q, v))
        for evt in waits:
            if evt is None:
                continue
            q, v = evt
            if v <= 0:
                continue
            if q in DMAQ:
                v = cnt.get(q, 0)
            elif selfv[eng].get(q, 0) >= v:
                continue
            if waited[eng].get(q, 0) >= v:
                continue
            waited[eng][q] = v
            ws.append((q, v))
        ev = None
        incs = []
        if inc is not None:
            cnt[inc] = cnt.get(inc, 0) + amt
            if inc not in DMAQ:
                selfv[eng][inc] = cnt[inc]
            incs.append((inc, amt))
            ev = (inc, cnt[inc])
        streams[eng].append((ws, emit, incs))
        return ev

    def dma(queue_eng, out_ap, in_ap, waits=(), inc="ld"):
        return op(queue_eng,
                  lambda e, o=out_ap, i=in_ap: e.dma_start(out=o, in_=i),
                  waits=waits, inc=inc, amt=16)

    def emax(evs):
        best = {}
        for evt in evs:
            if evt is None:
                continue
            q, v = evt
            if best.get(q, -1) < v:
                best[q] = v
        return [(q, v) for q, v in best.items()]

    tr = {
        "xs_free": [EV0, EV0],
        "xsb_free": [EV0, EV0],
        "x1b_free": [EV0, EV0],
        "xf_free": [EV0, EV0],
        "dt_free": [EV0, EV0],
        "sp_free": [EV0, EV0],
        "pb01_free": [EV0, EV0],
        "pb23_free": EV0,
        "dA_free": [EV0] * NDA,
        "u_free": [EV0] * NUR,
        "dtx_ev": [EV0, EV0],
        "dtx_free": [[], []],
        "carry": [[EV0] * NBLK, [EV0] * NBLK],
        "bbc_users": [],
        "zS_free": [EV0] * NBLK,
        "z_ev": [EV0] * NBLK,
        "sq_free": [EV0, EV0],
        "pb67_free": EV0,
        "pb45_free": EV0,
        "dtin_ev": [EV0] * NCH,
        "stats_pe": EV0,
        "dbg_gate": EV0,
        "hin_ev": EV0,
        "inj_free": EV0,
        "injdve_free": EV0,
    }

    stg_wxp = zS[:, 17:27, :].rearrange("p a t -> p (a t)").rearrange(
        "p (blk m) -> p blk m", m=160)
    stg_misc = zS[:, 31, :]
    stg_affn = zS[:, 0:8, :].rearrange("p a t -> p (a t)").rearrange(
        "p (blk m) -> p blk m", m=128)
    stg_bwt = zS[:, 8:16, :].rearrange("p a t -> p (a t)")
    stg_route = zS[:, 16, 0:256].rearrange("p (blk m) -> p blk m", m=8)

    ld_wxp = dma("gpsimd", stg_wxp,
                 wxp_h.rearrange("(blk p) m -> p blk m", p=128), inc="gd")
    ld_alog = dma("gpsimd", stg_misc, alog_h, inc="gd")
    ld_affn = dma("gpsimd", stg_affn,
                  affn_h.rearrange("(blk p) m -> p blk m", p=128), inc="gd")
    ld_bwt = dma("gpsimd", stg_bwt, bwt_h, inc="gd")
    ld_route = dma("gpsimd", stg_route,
                   route_h.rearrange("(blk p) m -> p blk m", p=128), inc="gd")
    ld_dv = dma("gpsimd", dvec_s[:], dvec_h, inc="gd")
    ld_bv = dma("gpsimd", bvec[:], bvec_h, inc="gd")
    ld_sb = dma("gpsimd", selB[:],
                selbc_h[:, 0:2048].rearrange("k (j m) -> k j m", m=128),
                inc="gd")
    ld_sc = dma("gpsimd", selC[:],
                selbc_h[:, 2048:4096].rearrange("k (j m) -> k j m", m=128),
                inc="gd")
    ld_se = dma("gpsimd", seler[:], seler_h, inc="gd")
    ld_o1 = dma("gpsimd", ones1[:], ones1_h, inc="gd")
    ld_m1 = dma("gpsimd", selm[:], selm_h, inc="gd")
    ld_m2 = dma("gpsimd", invm[:], invm_h, inc="gd")

    cast_wxp = op("scalar", lambda e: e.copy(wxpb[:], stg_wxp),
                  waits=[ld_wxp], inc="act")
    ev_negA = op("scalar", lambda e: e.activation(
        negA[:].rearrange("p a n -> p (a n)"), stg_misc, AF.Exp),
        waits=[ld_alog], inc="act")
    ev_negA = op("vector", lambda e: e.tensor_scalar(
        negA[:].rearrange("p a n -> p (a n)"),
        negA[:].rearrange("p a n -> p (a n)"), -1.0, None, ALU.mult),
        waits=[ev_negA], inc="dve")
    ev_oneD = op("scalar", lambda e: e.activation(
        oneD[:], dvec_s[:], AF.Copy, bias=1.0), waits=[ld_dv], inc="act")
    stg_wdt = zS[:, 17:25, :].rearrange("p a t -> p (a t)")
    ld_wdt = dma("gpsimd", stg_wdt, wdt_h, waits=[cast_wxp], inc="gd")
    cast_wdt = op("scalar", lambda e: e.copy(
        wdtb[:].rearrange("p a m -> p (a m)"), stg_wdt),
        waits=[ld_wdt], inc="act")
    ev_init = op("vector", lambda e: e.memset(hcarryA[:], 0.0), inc="dve")
    for _i in range(NDA):
        ev_init0 = op("vector", lambda e, _i=_i: e.memset(
            dA_q[_i][:, :, 0:1].rearrange("p a o -> p (a o)"), 0.0),
            inc="dve")
    ev_init2 = op("vector", lambda e: e.memset(rcarry[:], 0.0), inc="dve")
    ev_init3 = op("vector", lambda e: e.memset(onesk[:], 1.0), inc="dve")
    for b in range(NBLK):
        tr["carry"][0][b] = ev_init

    HLD = {"dtgp": EV0, "bbc": EV0, "cbc": EV0, "inj_free": EV0}

    def emit_sweep_block(c, b):
        r2 = b % 2
        lx = dma("sync", xs_rot[r2][:],
                 x_h[b * 128:(b + 1) * 128, c * TC:(c + 1) * TC],
                 waits=[tr["xs_free"][r2]], inc="ld")
        tr["xs_free"][r2] = op(
            "scalar", lambda e, r2=r2: e.copy(xsb_rot[r2][:], xs_rot[r2][:]),
            waits=[lx, tr["xsb_free"][r2]], inc="act")
        w = [tr["xs_free"][r2], cast_wxp]
        if b == 0:
            w.append(tr["pb23_free"])

        def mm(e, b=b):
            e.matmul(pb[2][:], wxpb[:, b, 0:128], xsb_rot[b % 2][:],
                     start=(b == 0), stop=(b == NBLK - 1))
            return e.matmul(pb[3][0:32, :], wxpb[:, b, 128:160],
                            xsb_rot[b % 2][:],
                            start=(b == 0), stop=(b == NBLK - 1))
        ev = op("tensor", mm, waits=w, inc="pe")
        tr["xsb_free"] = [ev, ev]
        if b == NBLK - 1:
            op("scalar", lambda e, c=c: e.copy(dtin_all[:, c, :], pb[2][:]),
               waits=[ev], inc="act")
            ev2 = op("scalar", lambda e, c=c: e.copy(bc_all[:, c, :],
                                                     pb[3][0:32, :]),
                     inc="act")
            tr["dtin_ev"][c] = ev2
            tr["pb23_free"] = ev2

    def emit_bcast(c, with_C):
        users = emax(tr["bbc_users"])
        tr["bbc_users"] = []
        nj = 32 if with_C else 16
        last = EV0
        for j in range(nj):
            bank = 2 + j % 2
            w = [tr["dtin_ev"][c]]
            if j < 2:
                w.append(tr["pb23_free"])
            if last is not None:
                w.append(last)

            def mm(e, j=j, bank=bank, c=c):
                sel = selC[:, j - 16, :] if j >= 16 else selB[:, j, :]
                return e.matmul(pb[bank][:], sel, bc_all[:, c, :],
                                start=True, stop=True)
            pe = op("tensor", mm, waits=w + ([ld_sb, ld_sc] if j == 0 else []),
                    inc="pe")

            def ev(e, j=j, bank=bank):
                dst = C_bc[:, j - 16, :] if j >= 16 else B_bc[:, j, :]
                return e.copy(dst, pb[bank][:])
            last = op("scalar", ev, waits=[pe] + (users if j == 0 else []),
                      inc="act")
        tr["pb23_free"] = last
        return last

    FMM = {}

    def emit_front_mm(c, b):
        r2 = b % 2
        lx = dma("sync", xf_rot[r2][:],
                 x_h[b * 128:(b + 1) * 128, c * TC:(c + 1) * TC],
                 waits=[tr["xf_free"][r2]], inc="ld")

        def mm_dt(e, c=c, b=b, r2=r2):
            return e.matmul(pb[r2][:], wdtb[:, b, :], dtin_all[:, c, :],
                            start=True, stop=True)
        dtmm = op("tensor", mm_dt,
                  waits=[tr["dtin_ev"][c], tr["pb01_free"][r2], cast_wdt],
                  inc="pe")
        FMM[b % 2] = (lx, dtmm)

    def emit_front(c, b, phase):
        r2 = b % 2
        lx, dtmm = FMM[r2]

        def sp1f(e, b=b, r2=r2):
            e.activation(sp1_rot[r2][:], pb[r2][:], AF.Abs,
                         bias=bvec[:, b:b + 1])
            return e.activation(dt_rot[r2][:], pb[r2][:], AF.Relu,
                                bias=bvec[:, b:b + 1],
                                accum_out=rs_acc[:, r2:r2 + 1])
        spa = op("scalar", sp1f,
                 waits=[dtmm, tr["dt_free"][r2], tr["sp_free"][r2], ld_bv],
                 inc="act")
        spb = op("scalar", lambda e, r2=r2: e.activation(
            pb[r2][:], sp1_rot[r2][:], AF.Exp, scale=-1.0),
            hw=[spa], inc="act")
        spv = op("scalar", lambda e, r2=r2: e.activation(
            sp1_rot[r2][:], pb[r2][:], AF.Ln, bias=1.0,
            accum_out=rs_acc[:, 2 + r2:3 + r2]),
            hw=[spb], inc="act")
        tr["pb01_free"][r2] = spv

        def gadd(e, b=b, r2=r2, phase=phase):
            i = e.tensor_tensor(dt_rot[r2][:], dt_rot[r2][:], sp1_rot[r2][:],
                                ALU.add)
            if phase == "A":
                e.tensor_tensor(rs_acc[:, r2:r2 + 1], rs_acc[:, r2:r2 + 1],
                                rs_acc[:, 2 + r2:3 + r2], ALU.add)
                i = e.tensor_tensor(rcarry[:, b:b + 1], rcarry[:, b:b + 1],
                                    rs_acc[:, r2:r2 + 1], ALU.add)
            return i
        gv = op("gpsimd", gadd, waits=[spv, ev_init2], inc="gp")
        tr["sp_free"][r2] = gv
        dtxv = op("gpsimd",
                  lambda e, r2=r2: e.tensor_tensor(
                      dtx_rot[r2][:], dt_rot[r2][:], xf_rot[r2][:], ALU.mult),
                  waits=[lx] + tr["dtx_free"][r2], inc="gp")
        tr["dt_free"][r2] = dtxv
        tr["dtx_ev"][r2] = dtxv
        if phase == "A":
            tr["xf_free"][r2] = dtxv
        HLD["dtgp"] = gv
        return gv

    def emit_scanwork(c, b, phase, gctr):
        r2 = b % 2
        carry_t = hcarryA if phase == "A" else Hin
        pidx = 0 if phase == "A" else 1
        carry_ev = tr["carry"][pidx][b]

        dA0v = op("scalar", lambda e, b=b, r2=r2: e.activation(
            inj[:, 0:16], negA[:, b, :], AF.Exp,
            scale=dt_rot[r2][:, 0:1]),
            waits=[HLD["dtgp"], ev_negA, tr["inj_free"]], inc="act")
        injA = op("gpsimd", lambda e, b=b, carry_t=carry_t: e.tensor_tensor(
            inj[:, 16:32], inj[:, 0:16], carry_t[:, b, :], ALU.mult),
            waits=[dA0v, carry_ev, tr["injdve_free"]], inc="gp")

        last_gp_u = EV0
        last_dve_u = EV0
        sev = EV0
        for q in range(QN):
            g = gctr[0]
            gctr[0] += 1
            s = g % NDA
            su = g % NUR

            def dA_emit(e, b=b, q=q, s=s, r2=r2):
                e.activation(dA_q[s][:, 0, 1:TC], dt_rot[r2][:, 1:TC],
                             AF.Exp, scale=negA[:, b, 2 * q:2 * q + 1])
                return e.activation(dA_q[s][:, 1, 1:TC], dt_rot[r2][:, 1:TC],
                                    AF.Exp,
                                    scale=negA[:, b, 2 * q + 1:2 * q + 2])
            dAev = op("scalar", dA_emit,
                      waits=[HLD["dtgp"], tr["dA_free"][s], ev_negA, ev_init0],
                      inc="act")

            on_dve = (phase == "A" and q >= QN - DVEQ_A)
            eng = "vector" if on_dve else "gpsimd"
            qn = "dve" if on_dve else "gp"

            def u_emit(e, q=q, su=su, r2=r2):
                e.tensor_tensor(u_q[su][:, 0, :], dtx_rot[r2][:],
                                B_bc[:, 2 * q, :], ALU.mult)
                return e.tensor_tensor(u_q[su][:, 1, :], dtx_rot[r2][:],
                                       B_bc[:, 2 * q + 1, :], ALU.mult)
            uev = op(eng, u_emit,
                     waits=[tr["dtx_ev"][r2], tr["u_free"][su], HLD["bbc"]],
                     inc=qn)

            def fix_emit(e, q=q, su=su):
                u0 = u_q[su][:, :, 0:1].rearrange("p a o -> p (a o)")
                return e.tensor_tensor(u0, u0, inj[:, 16 + 2 * q:18 + 2 * q],
                                       ALU.add)
            fev = op(eng, fix_emit, waits=[injA],
                     hw=[uev] if on_dve else (), inc=qn)
            if on_dve:
                last_dve_u = fev
            else:
                last_gp_u = fev

            def scan_emit(e, q=q, s=s, su=su):
                return e.tensor_tensor_scan(
                    p_bf[:, 2 * q:2 * q + 2, :].rearrange("p a t -> p (a t)"),
                    dA_q[s][:].rearrange("p a t -> p (a t)"),
                    u_q[su][:].rearrange("p a t -> p (a t)"),
                    0.0, ALU.mult, ALU.add)
            sev = op("vector", scan_emit,
                     waits=[uev, fev, dAev], inc="dve")
            tr["dA_free"][s] = sev
            tr["u_free"][su] = sev
        tr["inj_free"] = sev
        if last_dve_u is not None:
            tr["injdve_free"] = last_dve_u
        tr["dtx_free"][r2] = [ev for ev in (last_gp_u, last_dve_u)
                              if ev is not None]
        tr["bbc_users"].append(sev)

        cev = op("vector", lambda e, b=b, carry_t=carry_t: e.tensor_copy(
            carry_t[:, b, :],
            p_bf[:, :, TC - 1:TC].rearrange("p a o -> p (a o)")),
            waits=[tr["dbg_gate"]], hw=[sev], inc="dve")
        tr["carry"][pidx][b] = cev

        if phase == "B":
            def cm(e):
                e.tensor_tensor(p_bf[:, 0:8, :].rearrange("p a t -> p (a t)"),
                                p_bf[:, 0:8, :].rearrange("p a t -> p (a t)"),
                                C_bc[:, 0:8, :].rearrange("p a t -> p (a t)"),
                                ALU.mult)
                return e.tensor_tensor(
                    p_bf[:, 8:16, :].rearrange("p a t -> p (a t)"),
                    p_bf[:, 8:16, :].rearrange("p a t -> p (a t)"),
                    C_bc[:, 8:16, :].rearrange("p a t -> p (a t)"), ALU.mult)
            cmev = op("vector", cm, waits=[HLD["cbc"]], hw=[cev], inc="dve")
            tr["bbc_users"].append(cmev)
            t1 = op("vector", lambda e: e.tensor_tensor(
                p_bf[:, 0:8, :].rearrange("p a t -> p (a t)"),
                p_bf[:, 0:8, :].rearrange("p a t -> p (a t)"),
                p_bf[:, 8:16, :].rearrange("p a t -> p (a t)"), ALU.add),
                hw=[cmev], inc="dve")
            t2 = op("vector", lambda e: e.tensor_tensor(
                p_bf[:, 0:4, :].rearrange("p a t -> p (a t)"),
                p_bf[:, 0:4, :].rearrange("p a t -> p (a t)"),
                p_bf[:, 4:8, :].rearrange("p a t -> p (a t)"), ALU.add),
                hw=[t1], inc="dve")
            t3 = op("vector", lambda e: e.tensor_tensor(
                p_bf[:, 0:2, :].rearrange("p a t -> p (a t)"),
                p_bf[:, 0:2, :].rearrange("p a t -> p (a t)"),
                p_bf[:, 2:4, :].rearrange("p a t -> p (a t)"), ALU.add),
                hw=[t2], inc="dve")
            t4 = op("vector", lambda e, r2=r2: e.tensor_tensor(
                sq_rot[r2][:], p_bf[:, 0, :], p_bf[:, 1, :], ALU.add),
                waits=[tr["sq_free"][r2]], hw=[t3], inc="dve")
            zev = op("vector", lambda e, b=b, r2=r2: e.scalar_tensor_tensor(
                zS[:, b, :], xf_rot[r2][:], oneD[:, b:b + 1], sq_rot[r2][:],
                ALU.mult, ALU.add),
                waits=[ev_oneD, tr["zS_free"][b]], hw=[t4], inc="dve")
            tr["z_ev"][b] = zev
            tr["xf_free"][r2] = zev
            sqev = op("scalar", lambda e, b=b, r2=r2: e.activation(
                sq_rot[r2][:], zS[:, b, :], AF.Square),
                waits=[zev, tr["sq_free"][r2]], inc="act")

            def smm(e, b=b, r2=r2):
                e.matmul(pb[6][0:1, :], onesk[:], zS[:, b, :],
                         start=(b == 0), stop=(b == NBLK - 1))
                return e.matmul(pb[7][0:1, :], onesk[:], sq_rot[r2][:],
                                start=(b == 0), stop=(b == NBLK - 1))
            w = [sqev, ev_init3]
            if b == 0:
                w.append(tr["pb67_free"])
            pev = op("tensor", smm, waits=w, inc="pe")
            tr["sq_free"] = [pev, pev]
            if b == NBLK - 1:
                tr["stats_pe"] = pev

    def emit_stats_math():
        m1 = op("vector", lambda e: e.tensor_scalar(
            st_m, pb[6][0:1, :], 1.0 / DM, None, ALU.mult),
            waits=[tr["stats_pe"]], inc="dve")
        m2 = op("vector", lambda e: e.tensor_scalar(
            st_v, pb[7][0:1, :], 1.0 / DM, None, ALU.mult),
            hw=[m1], inc="dve")
        m3 = op("vector", lambda e: e.tensor_tensor(
            st_sc, st_m, st_m, ALU.mult), hw=[m2], inc="dve")
        m4 = op("vector", lambda e: e.tensor_tensor(
            st_v, st_v, st_sc, ALU.subtract), hw=[m3], inc="dve")
        m5 = op("vector", lambda e: e.tensor_scalar(
            st_v, st_v, EPS, None, ALU.add), hw=[m4], inc="dve")
        r1 = op("scalar", lambda e: e.activation(st_v, st_v, AF.Ln),
                waits=[m5], inc="act")
        r2 = op("scalar", lambda e: e.activation(st_v, st_v, AF.Exp,
                                                 scale=-0.5),
                hw=[r1], inc="act")
        mr = op("vector", lambda e: e.tensor_tensor(
            st_m, st_m, st_v, ALU.mult), waits=[r2], inc="dve")

        def bc(e):
            e.matmul(pb[4][:], ones1[:], st_v, start=True, stop=True)
            return e.matmul(pb[5][:], ones1[:], st_m, start=True, stop=True)
        bcp = op("tensor", bc, waits=[mr, tr["pb45_free"], ld_o1], inc="pe")
        return mr, bcp

    for b in range(NBLK):
        emit_sweep_block(0, b)
    if dbg:
        dma("gpsimd", dbgODm_h[0], oneD[:], waits=[tr["dtin_ev"][0]],
            inc="gd")
    for c in range(NCH):
        HLD["bbc"] = emit_bcast(c, with_C=False)
        gctr = [0]
        for b in range(NBLK):
            emit_front_mm(c, b)
            emit_front(c, b, "A")
            emit_scanwork(c, b, "A", gctr)
            if c + 1 < NCH:
                emit_sweep_block(c + 1, b)
        if dbg:
            if c == 0:
                dma("gpsimd", dbgODm_h[1], oneD[:],
                    waits=emax(tr["carry"][0]), inc="gd")
            if c == NCH - 1:
                dma("gpsimd", dbgODm_h[2], oneD[:],
                    waits=emax(tr["carry"][0]), inc="gd")
            dgev = dma("gpsimd", dbgSc_h[c],
                       hcarryA[:].rearrange("p a b -> p (a b)"),
                       waits=emax(tr["carry"][0]), inc="gd")
            tr["dbg_gate"] = dgev
            if c == 0:
                dma("gpsimd", dbgDT_h, dt_rot[1][:], waits=[dgev], inc="gd")
                dma("gpsimd", dbgDTX_h, dtx_rot[1][:], inc="gd")
                dma("gpsimd", dbgB_h,
                    B_bc[:].rearrange("p a t -> p (a t)"), inc="gd")
                dgev = dma("gpsimd", dbgP_h,
                           p_bf[:].rearrange("p a t -> p (a t)"), inc="gd")
                tr["dbg_gate"] = dgev

    pc_ev = EV0
    for b in range(NBLK):
        pc_ev = op("scalar", lambda e, b=b: e.activation(
            Pc[:, b, :], negA[:, b, :], AF.Exp, scale=rcarry[:, b:b + 1]),
            waits=([tr["sp_free"][0], tr["sp_free"][1]] if b == 0 else []),
            inc="act")
    cc1 = dma("gpsimd", cc_in[0:DM, :].rearrange("(blk p) n -> p blk n", p=128),
              hcarryA[:],
              waits=emax(tr["carry"][0] + [tr["dbg_gate"]]), inc="cc")
    cc2 = dma("gpsimd",
              cc_in[DM:2 * DM, :].rearrange("(blk p) n -> p blk n", p=128),
              Pc, waits=[pc_ev], inc="cc")
    ccv = op("gpsimd", lambda e: e.collective_compute(
        "AllGather", ALU.bypass, replica_groups=[list(range(NCORES))],
        ins=[cc_in.ap().opt()], outs=[cc_out.ap().opt()]),
        waits=[cc1, cc2], inc="cc")
    ccf = dma("gpsimd", ccf_in.ap(), cc_in[0:1, :], waits=[ccv], inc="cc")
    ccv = op("gpsimd", lambda e: e.collective_compute(
        "AllReduce", ALU.add, replica_groups=[list(range(NCORES))],
        ins=[ccf_in.ap().opt()], outs=[ccf_out.ap().opt()]),
        waits=[ccf], inc="cc")
    if dbg:
        dma("gpsimd", dbgS_h, hcarryA[:].rearrange("p a b -> p (a b)"),
            waits=[ccv], inc="gd")

    esW1.close()
    esW2 = contextlib.ExitStack()
    EW2 = esW2.enter_context
    gS1 = EW2(nc.sbuf_tensor("gS1", [128, NBLK * NST], F32, side="right"))
    gP1 = EW2(nc.sbuf_tensor("gP1", [128, NBLK * NST], F32, side="right"))
    pprime = EW2(nc.sbuf_tensor("pprime", [128, NBLK * NST], F32, side="right"))

    hv = op("vector", lambda e: e.memset(Hin[:], 0.0), waits=[ccv],
            inc="dve")
    Hf = Hin[:].rearrange("p a b -> p (a b)")
    for j in range(NCORES):
        base = j * 2 * DM
        gjs = dma("gpsimd",
                  gS1[:].rearrange("p (blk n) -> p blk n", n=NST),
                  cc_out[base:base + DM, :].rearrange(
                      "(blk p) n -> p blk n", p=128),
                  waits=[ccv, hv], inc="gd")
        gjp = dma("gpsimd",
                  gP1[:].rearrange("p (blk n) -> p blk n", n=NST),
                  cc_out[base + DM:base + 2 * DM, :].rearrange(
                      "(blk p) n -> p blk n", p=128), inc="gd")
        hv = op("vector", lambda e, j=j: e.tensor_scalar(
            pprime[:], gP1[:], selm[:, j:j + 1], invm[:, j:j + 1],
            ALU.mult, ALU.add), waits=[ld_m1, ld_m2, gjs, gjp], hw=[hv],
            inc="dve")
        hv = op("vector", lambda e: e.tensor_tensor(Hf, Hf, pprime[:],
                                                    ALU.mult),
                hw=[hv], inc="dve")
        hv = op("vector", lambda e, j=j: e.tensor_scalar(
            pprime[:], gS1[:], selm[:, j:j + 1], None, ALU.mult),
            hw=[hv], inc="dve")
        hv = op("vector", lambda e: e.tensor_tensor(Hf, Hf, pprime[:],
                                                    ALU.add),
                hw=[hv], inc="dve")
    hv = op("vector", lambda e: e.nop(), hw=[hv], inc="dve")
    tr["hin_ev"] = hv
    if dbg:
        dma("gpsimd", dbgH_h, Hin[:].rearrange("p a b -> p (a b)"),
            waits=[hv], inc="gd")
        dma("gpsimd", dbgODm_h[3], oneD[:], waits=[hv], inc="gd")

    esW2.close()
    esW3 = contextlib.ExitStack()
    EW3 = esW3.enter_context
    affnb = EW3(nc.sbuf_tensor("affnb", [128, NBLK, 128], BF16, side="right"))
    bwtb = EW3(nc.sbuf_tensor("bwtb", [128, NBLK, 128], BF16, side="right"))
    wc1 = op("scalar", lambda e: e.copy(affnb[:], stg_affn),
             waits=[ld_affn, hv], inc="act")
    wc2 = op("scalar", lambda e: e.copy(
        bwtb[:].rearrange("p a m -> p (a m)"), stg_bwt),
        waits=[ld_bwt, hv], inc="act")
    wc3 = op("scalar", lambda e: e.copy(routeb[:], stg_route),
             waits=[ld_route], inc="act")
    for b in range(NBLK):
        tr["zS_free"][b] = wc3
        tr["carry"][1][b] = hv

    for _e in ("tensor", "scalar", "vector", "gpsimd"):
        op(_e, lambda e: e.nop(), waits=[hv, ccv])

    for c in range(NCH):
        cb = emit_bcast(c, with_C=True)
        HLD["bbc"] = cb
        HLD["cbc"] = cb
        gctr = [0]
        emit_front_mm(c, 0)
        FMM2 = dict(FMM)
        for b in range(NBLK):
            FMM[b % 2] = FMM2[b % 2]
            if b + 1 < NBLK:
                emit_front_mm(c, b + 1)
                FMM2[(b + 1) % 2] = FMM[(b + 1) % 2]
                FMM[b % 2] = FMM2[b % 2]
            emit_front(c, b, "B")
            emit_scanwork(c, b, "B", gctr)

        dbgz_ev = EV0
        if dbg and c == 0:
            dbgz_ev = dma("gpsimd", dbgZ_h,
                          zS[:].rearrange("p a t -> p (a t)"),
                          waits=emax(tr["z_ev"]), inc="gd")
        mr1, bc1 = emit_stats_math()
        ln1 = dbgz_ev
        ffn_pe = EV0
        for b in range(NBLK):
            ln1 = op("vector", lambda e, b=b: e.tensor_tensor(
                zS[:, b, :], zS[:, b, :], pb[4][:], ALU.mult),
                waits=[bc1, tr["z_ev"][b]], hw=([ln1] if ln1 else ()),
                inc="dve")
            ln1 = op("vector", lambda e, b=b: e.tensor_tensor(
                zS[:, b, :], zS[:, b, :], pb[5][:], ALU.subtract),
                hw=[ln1], inc="dve")
            xc = op("scalar", lambda e, b=b: e.copy(x1b_rot[b % 2][:],
                                                    zS[:, b, :]),
                    waits=[ln1, tr["x1b_free"][b % 2]], inc="act")

            def fmm(e, b=b):
                e.matmul(pb[6][0:8, :], routeb[:, b, :], x1b_rot[b % 2][:],
                         start=(b == 0), stop=(b == NBLK - 1))
                return e.matmul(pb[7][:], affnb[:, b, :], x1b_rot[b % 2][:],
                                start=(b == 0), stop=(b == NBLK - 1))
            ffn_pe = op("tensor", fmm, waits=[xc, wc1, wc2, wc3], inc="pe")
            tr["x1b_free"][b % 2] = ffn_pe
        tr["pb45_free"] = ln1

        dbgx_ev = EV0
        if dbg and c == 0:
            dbgx_ev = dma("gpsimd", dbgX1_h,
                          zS[:].rearrange("p a t -> p (a t)"),
                          waits=[ln1], inc="gd")
        lge = op("scalar", lambda e: e.copy(lg_s[:], pb[6][0:8, :]),
                 waits=[ffn_pe], inc="act")
        if dbg and c == 0:
            lge = dma("gpsimd", dbgLG_h, lg_s[:], waits=[lge], inc="gd")
        s1 = dma("sync", tmp8[0:4, :], lg_s[4:8, :], waits=[lge], inc="sd")
        v1 = op("vector", lambda e: e.tensor_tensor(
            tmp8[0:4, :], lg_s[0:4, :], tmp8[0:4, :], ALU.max),
            waits=[s1], inc="dve")
        s2 = dma("sync", hf_s[0:2, :], tmp8[2:4, :], waits=[v1], inc="sd")
        v2 = op("vector", lambda e: e.tensor_tensor(
            tmp8[0:2, :], tmp8[0:2, :], hf_s[0:2, :], ALU.max),
            waits=[s2], inc="dve")
        s3 = dma("sync", hf_s[0:1, :], tmp8[1:2, :], waits=[v2], inc="sd")
        v3 = op("vector", lambda e: e.tensor_tensor(
            tmp8[0:1, :], tmp8[0:1, :], hf_s[0:1, :], ALU.max),
            waits=[s3], inc="dve")
        m8 = op("tensor", lambda e: e.matmul(
            pb[4][0:8, :], ones1[:, 0:8], tmp8[0:1, :], start=True, stop=True),
            waits=[v3], inc="pe")
        v4 = op("vector", lambda e: e.tensor_tensor(
            tmp8[:], lg_s[:], pb[4][0:8, :], ALU.is_equal),
            waits=[m8], inc="dve")
        v5 = op("vector", lambda e: e.tensor_tensor(
            tmp8[:], tmp8[:], lg_s[:], ALU.mult), hw=[v4], inc="dve")
        if dbg and c == 0:
            v5 = dma("gpsimd", dbgRW_h, tmp8[:], waits=[v5], inc="gd")
        mkp = op("tensor", lambda e: e.matmul(
            pb[5][:], seler[:], tmp8[:], start=True, stop=True),
            waits=[v5, ld_se], inc="pe")
        hfe = op("scalar", lambda e: e.copy(hf_s[:], pb[7][:]),
                 waits=[mkp], inc="act")
        whv = op("vector", lambda e: e.tensor_tensor(
            hf_s[:], hf_s[:], pb[5][:], ALU.mult), waits=[hfe], inc="dve")
        if dbg and c == 0:
            whv = dma("gpsimd", dbgWH_h, hf_s[:], waits=[whv], inc="gd")
        whc = op("scalar", lambda e: e.copy(whb[:], hf_s[:]),
                 waits=[whv], inc="act")
        if dbg and c == 0:
            z2p = [whv, whv]

        z2p = [whv, whv]
        sq2e = [EV0, EV0]
        for b in range(NBLK):
            bank = 4 + b % 2

            def fo(e, b=b, bank=bank):
                return e.matmul(pb[bank][:], bwtb[:, b, :], whb[:],
                                start=True, stop=True)
            fop = op("tensor", fo, waits=[whc, z2p[b % 2]], inc="pe")
            z2 = op("vector", lambda e, b=b, bank=bank: e.tensor_tensor(
                zS[:, b, :], zS[:, b, :], pb[bank][:], ALU.add),
                waits=[fop], inc="dve")
            z2p[b % 2] = z2
            sq2 = op("scalar", lambda e, b=b: e.activation(
                sq_rot[b % 2][:], zS[:, b, :], AF.Square),
                waits=[z2, sq2e[b % 2]], inc="act")

            def smm2(e, b=b):
                e.matmul(pb[6][0:1, :], onesk[:], zS[:, b, :],
                         start=(b == 0), stop=(b == NBLK - 1))
                return e.matmul(pb[7][0:1, :], onesk[:], sq_rot[b % 2][:],
                                start=(b == 0), stop=(b == NBLK - 1))
            st2 = op("tensor", smm2, waits=[sq2], inc="pe")
            sq2e = [st2, st2]
            if b == NBLK - 1:
                tr["stats_pe"] = st2
        tr["sq_free"] = [tr["stats_pe"], tr["stats_pe"]]

        dbgz2_ev = EV0
        if dbg and c == 0:
            dbgz2_ev = dma("gpsimd", dbgZ2_h,
                           zS[:].rearrange("p a t -> p (a t)"),
                           waits=emax(z2p), inc="gd")
        mr2, bc2 = emit_stats_math()
        tr["pb67_free"] = mr2
        ln2 = dbgz2_ev
        for b in range(NBLK):
            ln2 = op("vector", lambda e, b=b: e.tensor_tensor(
                zS[:, b, :], zS[:, b, :], pb[4][:], ALU.mult),
                waits=[bc2], hw=([ln2] if ln2 else ()), inc="dve")
            ln2 = op("vector", lambda e, b=b: e.tensor_tensor(
                zS[:, b, :], zS[:, b, :], pb[5][:], ALU.subtract),
                hw=[ln2], inc="dve")
            stv = dma("sync",
                      out_h[b * 128:(b + 1) * 128, c * TC:(c + 1) * TC],
                      zS[:, b, :], waits=[ln2], inc="st")
            tr["zS_free"][b] = stv
        tr["pb45_free"] = ln2
        bwaits = [(q, cnt.get(q, 0)) for q in ("pe", "act", "dve", "gp")]
        for _e in ("tensor", "scalar", "vector", "gpsimd"):
            op(_e, lambda e: e.nop(), waits=list(bwaits))
        if dbg and c == 0:
            dma("gpsimd", dbgODm_h[4], oneD[:], waits=[ln2], inc="gd")

    if dbg:
        dma("gpsimd", dbgOD_h, oneD[:],
            waits=[("st", cnt.get("st", 0))], inc="gd")
        dma("gpsimd", dbgDV_h, dvec_s[:], inc="gd")

    op("gpsimd", lambda e: e.nop(),
       waits=[("st", cnt.get("st", 0)), ("gd", cnt.get("gd", 0)),
              ("ld", cnt.get("ld", 0)), ("sd", cnt.get("sd", 0)),
              ("cc", cnt.get("cc", 0))])

    sems = {q: E(nc.semaphore("q_" + q)) for q in sorted(cnt)}
    blk = E(nc.Block())
    for eng in ENG:
        sec = getattr(blk, eng)

        def body(e, eng=eng):
            for (ws, emit, incs) in streams[eng]:
                for (q, v) in ws:
                    e.wait_ge(sems[q], v)
                inst = emit(e)
                for (q, amt) in incs:
                    inst.then_inc(sems[q], amt)
        sec(body)

    esW3.close()
    es.close()
    return nc


def _host_prep(L0_attn_x, L0_attn_A_log, L0_attn_D, L0_attn_x_proj_w,
               L0_attn_dt_proj_w, L0_attn_dt_proj_b, L0_ffn_route_w, L0_ffn_A,
               L0_ffn_Bw):
    import ml_dtypes
    x = np.asarray(L0_attn_x, dtype=np.float32)
    A_log = np.asarray(L0_attn_A_log, dtype=np.float32)
    Dv = np.asarray(L0_attn_D, dtype=np.float32)
    xp_w = np.asarray(L0_attn_x_proj_w, dtype=np.float32)
    dt_w = np.asarray(L0_attn_dt_proj_w, dtype=np.float32)
    dt_b = np.asarray(L0_attn_dt_proj_b, dtype=np.float32)
    route_w = np.asarray(L0_ffn_route_w, dtype=np.float32)
    A_ffn = np.asarray(L0_ffn_A, dtype=np.float32)
    Bw = np.asarray(L0_ffn_Bw, dtype=np.float32)

    wxp = np.ascontiguousarray(xp_w.T)
    wdt = np.ascontiguousarray(dt_w.T)
    alog = np.ascontiguousarray(
        A_log.reshape(NBLK, 128, NST).transpose(1, 0, 2)).reshape(
        128, NBLK * NST)
    dvec = np.ascontiguousarray(Dv.reshape(NBLK, 128).T)
    bvec = np.ascontiguousarray(dt_b.reshape(NBLK, 128).T)
    route = np.ascontiguousarray(route_w.T)
    affn = np.ascontiguousarray(np.transpose(A_ffn, (2, 0, 1)).reshape(DM, 128))
    bwt = np.ascontiguousarray(np.transpose(Bw, (0, 2, 1)).reshape(128, DM))
    selbc = np.zeros((32, 32 * 128), dtype=ml_dtypes.bfloat16)
    for j in range(32):
        selbc[j, j * 128:(j + 1) * 128] = 1.0
    seler = np.zeros((8, 128), dtype=np.float32)
    for e in range(8):
        seler[e, e * 16:(e + 1) * 16] = 1.0
    ones1 = np.ones((1, 128), dtype=np.float32)

    in_maps = []
    for cid in range(NCORES):
        bb, qq = cid // 4, cid % 4
        xc = np.ascontiguousarray(x[bb, qq * TLOC:(qq + 1) * TLOC, :].T)
        sel = np.zeros(8, dtype=np.float32)
        for j in range(NCORES):
            if j // 4 == bb and j % 4 < qq:
                sel[j] = 1.0
        in_maps.append({
            "x": xc, "wxp": wxp, "wdt": wdt, "alog": alog, "dvec": dvec,
            "bvec": bvec, "route": route, "affn": affn, "bwt": bwt,
            "selbc": selbc, "seler": seler, "ones1": ones1,
            "selm": np.ascontiguousarray(np.tile(sel, (128, 1))),
            "invm": np.ascontiguousarray(np.tile(1.0 - sel, (128, 1))),
        })
    return in_maps


def kernel(L0_attn_x, L0_attn_A_log, L0_attn_D, L0_attn_x_proj_w,
           L0_attn_dt_proj_w, L0_attn_dt_proj_b, L0_ffn_route_w, L0_ffn_A,
           L0_ffn_Bw):
    in_maps = _host_prep(L0_attn_x, L0_attn_A_log, L0_attn_D, L0_attn_x_proj_w,
                         L0_attn_dt_proj_w, L0_attn_dt_proj_b, L0_ffn_route_w,
                         L0_ffn_A, L0_ffn_Bw)
    nc = build_nc()
    res = run_bass_kernel_spmd(nc, in_maps, core_ids=list(range(NCORES)))
    out = np.empty((2, 8192, DM), dtype=np.float32)
    for cid in range(NCORES):
        bb, qq = cid // 4, cid % 4
        out[bb, qq * TLOC:(qq + 1) * TLOC, :] = res.results[cid]["o"].T
    return out
